# version 20
# baseline (speedup 1.0000x reference)
import numpy as np
import ml_dtypes
from contextlib import ExitStack

import concourse.bass as bass
import concourse.mybir as mybir
from concourse.bass_utils import run_bass_kernel_spmd

F32 = mybir.dt.float32
BF16 = mybir.dt.bfloat16
I8 = mybir.dt.int8
AF = mybir.ActivationFunctionType
ALU = mybir.AluOpType

CFG_FULL = dict(D=2048, S=4096, INTER=8192, NCORES=8)

RSQRT_DK = 1.0 / float(np.sqrt(128.0))

ENGS = ("sy", "te", "ve", "sc", "gp")


class Prog:

    def __init__(self):
        self.ops = {e: [] for e in ENGS}
        self.cnt = {}
        self.seen = {e: {} for e in ENGS}

    def sem(self, name):
        if name not in self.cnt:
            self.cnt[name] = 0

    def wait(self, eng, ev):
        if ev is None:
            return
        name, v = ev
        self.sem(name)
        if self.seen[eng].get(name, -1) >= v:
            return
        self.seen[eng][name] = v
        self.ops[eng].append(("w", name, v))

    def waits(self, eng, evs):
        for ev in evs:
            self.wait(eng, ev)

    def op(self, eng, fn, inc=None, n=1, waits=()):
        self.waits(eng, waits)
        ev = None
        if inc is not None:
            self.sem(inc)
            self.cnt[inc] += n
            ev = (inc, self.cnt[inc])
        self.ops[eng].append(("i", fn, inc, n))
        return ev

    def dma(self, eng, out, in_, sem, waits=()):
        return self.op(eng, (lambda e, o=out, i=in_: e.dma_start(out=o, in_=i)),
                       inc=sem, n=16, waits=waits)

    def final(self, name):
        self.sem(name)
        return (name, self.cnt[name])

    def bar(self):
        return [(s, self.cnt[s]) for s in ("pe", "ve", "sc", "gp")
                if self.cnt.get(s, 0) > 0]


def own_rows(cfg, hh):
    S = cfg["S"]
    rows = []
    for q in range(4):
        base = q * (S // 4) + hh * (S // 8)
        rows.extend(range(base, base + S // 8))
    return np.array(rows)


def make_host_inputs(cfg, x, wq, wk, wv, wo, wg, wu, wd, sg, su, sd):
    D, S, INTER, NC = cfg["D"], cfg["S"], cfg["INTER"], cfg["NCORES"]
    Dh = D // 2
    dk = 128
    half = dk // 2
    SH = S // 2

    inv_freq = (10000.0) ** (-2.0 * np.arange(half, dtype=np.float32) / dk)
    pos = np.arange(S, dtype=np.float32)
    ang = pos[:, None] * inv_freq[None, :]
    c = np.cos(ang).astype(np.float32)
    s = np.sin(ang).astype(np.float32)
    cos_tm = np.empty((S, dk), np.float32)
    cos_tm[:, 0::2] = c
    cos_tm[:, 1::2] = c
    sin_sgn = np.empty((S, dk), np.float32)
    sin_sgn[:, 0::2] = -s
    sin_sgn[:, 1::2] = s

    kk, qq = np.meshgrid(np.arange(128), np.arange(128), indexing="ij")
    maskT = (kk <= qq).astype(np.float32)
    maskadd = np.where(kk > qq, -1.0e30, 0.0).astype(np.float32).T.copy()
    idn = np.eye(128, dtype=np.float32)

    sg2d = np.ascontiguousarray(sg.reshape(INTER // 128, 128).T).astype(np.float32)
    su2d = np.ascontiguousarray(su.reshape(INTER // 128, 128).T).astype(np.float32)
    sd2d = np.ascontiguousarray(sd.reshape(D // 128, 128).T).astype(np.float32)

    wqT = wq.T
    wkT = wk.T
    wvT = wv.T
    woT = wo.T

    in_maps = []
    for core in range(NC):
        b = core // 2
        hh = core % 2
        jsl = slice(hh * Dh, (hh + 1) * Dh)
        in_maps.append({
            "xT": np.ascontiguousarray(x[b].T),
            "xh": np.ascontiguousarray(x[b][own_rows(cfg, hh), :]),
            "wqT": np.ascontiguousarray(wqT[:, jsl]),
            "wkT": np.ascontiguousarray(wkT[:, jsl]),
            "wvT": np.ascontiguousarray(wvT[:, jsl]),
            "woT": np.ascontiguousarray(woT[jsl, :]),
            "wg": wg, "wu": wu, "wd": wd,
            "sg2d": sg2d, "su2d": su2d, "sd2d": sd2d,
            "cos_tm": cos_tm, "sin_sgn": sin_sgn,
            "maskT": maskT, "maskadd": maskadd, "idn": idn,
        })
    return in_maps


def assemble_output(cfg, results):
    D, S, NC = cfg["D"], cfg["S"], cfg["NCORES"]
    SH = S // 2
    out = np.empty((NC // 2, S, D), np.float32)
    for core in range(NC):
        b = core // 2
        hh = core % 2
        out[b, own_rows(cfg, hh), :] = results[core]["outT"].T
    return out


def build_program(cfg, stop_after='D', ablate=()):
    D, S, INTER, NC = cfg["D"], cfg["S"], cfg["INTER"], cfg["NCORES"]
    Dh = D // 2
    HH = Dh // 128
    KC = D // 128
    TT = S // 128
    TGS = S // 512
    JB = min(512, Dh)
    NJB = Dh // JB
    JBH = JB // 128
    NG = TT // 4
    SH = S // 2
    TH = SH // 128
    QG = SH // 512
    ND4 = D // 512
    ICB = INTER // 128
    PPW = 256
    PPD = 64

    nc = bass.Bass("TRN2", target_bir_lowering=False)

    xT = nc.dram_tensor("xT", [D, S], F32, kind="ExternalInput").ap()
    xh = nc.dram_tensor("xh", [SH, D], F32, kind="ExternalInput").ap()
    wqT = nc.dram_tensor("wqT", [D, Dh], F32, kind="ExternalInput").ap()
    wkT = nc.dram_tensor("wkT", [D, Dh], F32, kind="ExternalInput").ap()
    wvT = nc.dram_tensor("wvT", [D, Dh], F32, kind="ExternalInput").ap()
    woT = nc.dram_tensor("woT", [Dh, D], F32, kind="ExternalInput").ap()
    wg = nc.dram_tensor("wg", [D, INTER], I8, kind="ExternalInput").ap()
    wu = nc.dram_tensor("wu", [D, INTER], I8, kind="ExternalInput").ap()
    wd = nc.dram_tensor("wd", [INTER, D], I8, kind="ExternalInput").ap()
    sg2d = nc.dram_tensor("sg2d", [128, INTER // 128], F32, kind="ExternalInput").ap()
    su2d = nc.dram_tensor("su2d", [128, INTER // 128], F32, kind="ExternalInput").ap()
    sd2d = nc.dram_tensor("sd2d", [128, D // 128], F32, kind="ExternalInput").ap()
    cos_tm = nc.dram_tensor("cos_tm", [S, 128], F32, kind="ExternalInput").ap()
    sin_sgn = nc.dram_tensor("sin_sgn", [S, 128], F32, kind="ExternalInput").ap()
    maskT = nc.dram_tensor("maskT", [128, 128], F32, kind="ExternalInput").ap()
    maskadd = nc.dram_tensor("maskadd", [128, 128], F32,
                             kind="ExternalInput").ap()
    idn = nc.dram_tensor("idn", [128, 128], F32, kind="ExternalInput").ap()
    outT = nc.dram_tensor("outT", [D, SH], F32, kind="ExternalOutput").ap()

    Qs = nc.dram_tensor("Qs", [Dh, S], BF16, kind="Internal").ap()
    Ks = nc.dram_tensor("Ks", [Dh, S], BF16, kind="Internal").ap()
    Vs = nc.dram_tensor("Vs", [128, HH * TT * 128], BF16, kind="Internal").ap()
    WGs = nc.dram_tensor("WGs", [128, ICB * KC * 128], BF16,
                         kind="Internal").ap()
    WUs = nc.dram_tensor("WUs", [128, ICB * KC * 128], BF16,
                         kind="Internal").ap()
    WDs = nc.dram_tensor("WDs", [128, KC * ICB * 128], BF16,
                         kind="Internal").ap()
    X2s = nc.dram_tensor("X2s", [128, (SH // 512) * KC * 512], BF16,
                         kind="Internal").ap()
    YP = nc.dram_tensor("YP", [S, D], BF16, kind="Internal").ap()
    RROWD = nc.dram_tensor("RROWD", [2, 512], F32, kind="Internal").ap()
    NQKD = nc.dram_tensor("NQKD", [128, 16], F32, kind="Internal").ap()
    CND = nc.dram_tensor("CND", [1, 8], F32, kind="Internal").ap()
    YSH = nc.dram_tensor("YSH", [SH, D], BF16, kind="Internal").ap()

    P = Prog()
    for s in ("pe", "ve", "sc", "gp"):
        P.sem(s)
    es = ExitStack()
    SBt = lambda n, sh, dt: es.enter_context(nc.sbuf_tensor(n, sh, dt)).ap()
    PSB = lambda n, sh, dt: es.enter_context(nc.psum_tensor(n, sh, dt)).ap()

    with es:
        arb = SBt("arb", [128, 98304], I8)
        arc = SBt("arc", [128, 65536], I8)
        ard = SBt("ard", [128, 35840], I8)
        ones_bf = SBt("ones_bf", [128, 2], BF16)
        negones_bf = SBt("negones_bf", [1, 128], BF16)
        onescol_f = SBt("onescol_f", [128, 2], F32)
        onesrow_f = SBt("onesrow_f", [1, 128], F32)
        epscol = SBt("epscol", [128, 2], F32)
        idn_bf = SBt("idn_bf", [128, 128], BF16)
        mask_bf = SBt("mask_bf", [128, 128], BF16)
        maska_bf = SBt("maska_bf", [128, 128], BF16)
        cst_f = SBt("cst_f", [128, 128], F32)
        sgc = SBt("sgc", [128, INTER // 128], F32)
        suc = SBt("suc", [128, INTER // 128], F32)
        sdc = SBt("sdc", [128, D // 128], F32)
        negc = SBt("negc", [128, max(HH, 2)], F32)
        nrowq = SBt("nrowq", [1, TGS], F32)
        nrowk = SBt("nrowk", [1, TGS], F32)
        t11a = SBt("t11a", [1, 2], F32)
        t11b = SBt("t11b", [1, 2], F32)
        c11bf = SBt("c11bf", [1, 2], BF16)
        srow = SBt("srow", [1, 512], F32)
        rrow = SBt("rrow", [1, 512], F32)
        rrowB = [SBt("rrowB0", [1, 512], F32), SBt("rrowB1", [1, 512], F32)]
        msq = SBt("msq", [1, 512], F32)
        murow = srow
        varow = rrow

        PSALL = PSB("PSALL", [128, 4096], F32)
        PS = [PSALL[:, i * 512:(i + 1) * 512] for i in range(6)]
        PSR = PSALL[:, 6 * 512:7 * 512]
        PST_ = PSALL[:, 7 * 512:8 * 512].bitcast(BF16)
        PSTH = [PST_[:, 0:512], PSR.bitcast(BF16)[:, 0:512]]

        class Carve:
            def __init__(self, ar):
                self.ar, self.off, self.limit = ar, 0, ar.shape[1]

            def take(self, elems, dt):
                sz = {F32: 4, BF16: 2, I8: 1}[dt]
                self.off = (self.off + 63) // 64 * 64
                v = self.ar[:, self.off:self.off + elems * sz].bitcast(dt)
                self.off += elems * sz
                assert self.off <= self.limit, (self.off, self.limit)
                return v

        ev_ps0 = P.op("ve", lambda e: e.memset(PSALL[:, :], 0.0), inc="ve")
        evc = P.dma("sy", cst_f[:, :], idn[:, :], "cs0")
        P.op("ve", lambda e: e.memset(ones_bf[:, :], 1.0), inc="ve")
        P.op("ve", lambda e: e.memset(negones_bf[:, :], -1.0), inc="ve")
        P.op("ve", lambda e: e.memset(onescol_f[:, :], 1.0), inc="ve")
        P.op("ve", lambda e: e.memset(onesrow_f[:, :], 1.0), inc="ve")
        P.op("ve", lambda e: e.memset(epscol[:, :], 1e-5), inc="ve")
        ev_idn = P.op("ve", lambda e: e.tensor_copy(idn_bf[:, :], cst_f[:, :]),
                      inc="ve", waits=[evc])
        evc2 = P.dma("sy", cst_f[:, :], maskT[:, :], "cs1", waits=[ev_idn])
        ev_mask = P.op("ve", lambda e: e.tensor_copy(mask_bf[:, :], cst_f[:, :]),
                       inc="ve", waits=[evc2])
        evc3 = P.dma("sy", cst_f[:, :], maskadd[:, :], "cs5", waits=[ev_mask])
        ev_maska = P.op("ve", lambda e: e.tensor_copy(maska_bf[:, :], cst_f[:, :]),
                        inc="ve", waits=[evc3])
        ev_sg = P.dma("sy", sgc[:, :], sg2d[:, :], "cs2")
        ev_su = P.dma("sy", suc[:, :], su2d[:, :], "cs3")
        ev_sd = P.dma("sy", sdc[:, :], sd2d[:, :], "cs4")

        cb = Carve(arb)
        wq_sb = cb.take(KC * Dh, BF16)
        wk_sb = cb.take(KC * Dh, BF16)
        wv_sb = cb.take(KC * Dh, BF16)
        cc_ = Carve(arc)
        wst2 = [cc_.take((KC // 4) * Dh, F32) for _ in range(2)]
        qkr = [cc_.take(512, BF16) for _ in range(3)]
        trev = [cc_.take(512, BF16) for _ in range(3)]
        vev = [cc_.take(512, BF16) for _ in range(2)]
        sqstrip = cc_.take(2 * HH * TT, F32)
        sqtrash = cc_.take(128, F32)
        nqk_sb = cc_.take(2 * HH, F32)
        mqh = cc_.take(128, F32)[0:8, :]
        mkh = cc_.take(128, F32)[0:8, :]
        mq1 = cc_.take(2, F32)[0:8, :]
        mk1 = cc_.take(2, F32)[0:8, :]
        ppc = Carve(arb)
        ppc.off = 65536
        ppi8 = [ppc.take(4096, I8) for _ in range(2)]
        ppbf = [ppc.take(4096, BF16) for _ in range(2)]
        cd = Carve(ard)
        xst = [cd.take(KC * 128, F32) for _ in range(2)]
        xbf = [cd.take(KC * 128, BF16) for _ in range(2)]
        ctile = [cd.take(128, F32) for _ in range(2)]
        stile = [cd.take(128, F32) for _ in range(2)]
        t1s = [cd.take(512, F32) for _ in range(2)]
        t2s = [cd.take(512, F32) for _ in range(2)]

        ev_wcast = None
        wst_free = [None, None]
        wi = 0
        for wdr, wsb in ((wqT, wq_sb), (wkT, wk_sb), (wvT, wv_sb)):
            for hf in range(4):
                wsl = wi % 2
                wi += 1
                cs = slice(hf * (KC // 4), (hf + 1) * (KC // 4))
                src = wdr.rearrange("(c p) j -> p c j", p=128)[:, cs, :]
                evl = P.dma("sy", wst2[wsl].rearrange("p (c j) -> p c j", j=Dh),
                            src, f"wl{wsl}", waits=[wst_free[wsl]])
                dst = wsb.rearrange("p (c j) -> p c j", j=Dh)[:, cs, :]
                ev_wcast = P.op(
                    "ve", lambda e, d=dst, s_=wst2[wsl]: e.tensor_copy(
                        d, s_.rearrange("p (c j) -> p c j", j=Dh)),
                    inc="ve", waits=[evl])
                wst_free[wsl] = ev_wcast
        ev_wstfree = ev_wcast

        pp_chunks = []
        for wsrc, wdst, w_cols, cw in ((wg, WGs, INTER, PPW), (wu, WUs, INTER, PPW),
                                       (wd, WDs, D, PPD)):
            nch = wsrc.shape[0] // 128
            w4 = wdst.rearrange("p (b c u) -> p b c u", c=nch, u=128)
            for c0 in range(0, w_cols, cw):
                src = wsrc.rearrange("(c p) i -> p c i", p=128)[:, :, c0:c0 + cw]
                if cw >= 128:
                    nb = cw // 128
                    b0 = c0 // 128
                    dsts = [(wdst[:, (b0 + b) * nch * 128:
                                  (b0 + b + 1) * nch * 128],
                             (lambda buf, el, n=nch, bb=nb, b_=b:
                              buf[:, 0:el].rearrange(
                                  "p (c b u) -> p c b u", c=n, b=bb,
                                  u=128)[:, :, b_, :]))
                            for b in range(nb)]
                else:
                    u0 = c0 % 128
                    dsts = [(w4[:, c0 // 128, :, u0:u0 + cw],
                             (lambda buf, el, n=nch: buf[:, 0:el].rearrange(
                                 "p (c i) -> p c i", c=n)))]
                pp_chunks.append((src, dsts, nch * cw, nch))
        pp_state = dict(i=0, ev_cast=[None, None], ev_spill=[None, None],
                        bar0=None)

        def pump_prepass(nsteps):
            for _ in range(nsteps):
                i = pp_state["i"]
                if i >= len(pp_chunks):
                    return
                src, dsts, elems, nch = pp_chunks[i]
                sl = i % 2
                evl = P.dma("sy", ppi8[sl][:, 0:elems].rearrange(
                    "p (c i) -> p c i", c=nch), src, f"pp{sl}",
                    waits=[pp_state["ev_cast"][sl]] +
                          (pp_state.get("bar0") or [] if i < 2 else []))
                evcst = P.op("gp", lambda e, a=ppbf[sl][:, 0:elems],
                             b=ppi8[sl][:, 0:elems]: e.tensor_copy(a, b),
                             inc="gp", waits=[evl, pp_state["ev_spill"][sl]])
                pp_state["ev_cast"][sl] = evcst
                evs = None
                for dst, view in dsts:
                    evs = P.dma("gp", dst, view(ppbf[sl], elems), f"pq{sl}",
                                waits=[evcst])
                pp_state["ev_spill"][sl] = evs
                pp_state["i"] = i + 1

        bank_rel = {i: None for i in range(6)}
        pst_rel = [None, None]
        pend_tr = []
        slot_rel = dict(t1=[None] * 2, t2=[None] * 2, qkr=[None] * 3,
                        xst=[None, None], xbf=[None, None])
        sq_slot = [None] * 3
        trv_sem = [None] * 3
        vev_sem = [None, None]
        rope_ct = 0
        tr_ct = [0]
        vev_ct = 0

        xT3 = xT.rearrange("(c p) t -> p c t", p=128)
        cos3 = cos_tm.rearrange("(tt p) j -> p tt j", p=128)
        sin3 = sin_sgn.rearrange("(tt p) j -> p tt j", p=128)

        for t in range(TT):
            sl = t % 2
            evx = P.dma("sy", xst[sl].rearrange("p (c u) -> p c u", u=128),
                        xT3[:, :, t * 128:(t + 1) * 128], f"xs{sl}",
                        waits=[slot_rel["xst"][sl]])
            evcos = P.dma("sy", ctile[sl][:, :], cos3[:, t, :], f"xs{sl}")
            evsin = P.dma("sy", stile[sl][:, :], sin3[:, t, :], f"xs{sl}")
            evcast = P.op("ve", lambda e, a=xbf[sl], b=xst[sl]: e.tensor_copy(a, b),
                          inc="ve", waits=[evsin, slot_rel["xbf"][sl]])

            for jb in range(NJB):
                bs = ((t * NJB + jb) % 2) * 3
                qb, kb, vb = PS[bs], PS[bs + 1], PS[bs + 2]
                P.waits("te", [ev_ps0, evcast, ev_wcast, bank_rel[bs],
                               bank_rel[bs + 1], bank_rel[bs + 2]])
                evq = evk = evv = None
                for kc in range(KC):
                    lhs = xbf[sl].rearrange("p (c u) -> p c u", u=128)[:, kc, :]
                    st_ = (kc == 0)
                    sp_ = (kc == KC - 1)
                    for wi, (bnk, wsb) in enumerate(((qb, wq_sb), (kb, wk_sb),
                                                     (vb, wv_sb))):
                        rhs = wsb.rearrange("p (c j) -> p c j", j=Dh)[
                            :, kc, jb * JB:(jb + 1) * JB]
                        ev = P.op("te", lambda e, o=bnk[:, 0:JB], l=lhs, r=rhs,
                                  a=st_, b=sp_: e.matmul(o, l, r, start=a, stop=b),
                                  inc="pe" if sp_ else None)
                        if sp_:
                            if wi == 0:
                                evq = ev
                            elif wi == 1:
                                evk = ev
                            else:
                                evv = ev

                for pi, (bnk, evp, dstD) in enumerate(((qb, evq, Qs), (kb, evk, Ks))):
                    s2 = rope_ct % 3
                    rope_ct += 1
                    cosb = ctile[sl].rearrange("p j -> p () j").broadcast_to(
                        (128, JBH, 128))
                    sinb = stile[sl].rearrange("p j -> p () j").broadcast_to(
                        (128, JBH, 128))
                    if "norope" in ablate:
                        ev1 = P.op("ve", lambda e, o=t1s[s2][:, 0:JB],
                                   i=bnk[:, 0:JB]: e.tensor_copy(o, i),
                                   inc="ve", waits=[evp, slot_rel["t1"][s2]])
                        ev2 = P.op("ve", lambda e, o=t2s[s2][:, 0:JB],
                                   i=bnk[:, 0:JB]: e.tensor_copy(o, i),
                                   inc="ve", waits=[slot_rel["t2"][s2]])
                    else:
                        s3_ = rope_ct % 2
                        ev1 = P.op("ve", lambda e, o=t1s[s3_][:, 0:JB], i=bnk[:, 0:JB],
                                   cb_=cosb: e.scalar_tensor_tensor(
                                       o, i, 1.0, cb_, ALU.bypass, ALU.mult),
                                   inc="ve", waits=[evp, slot_rel["t1"][s3_]])
                        swap = bnk[:, 0:JB].rearrange("p (a b) -> p a b", b=2)[:, :, ::-1]
                        ev2 = P.op("ve", lambda e, o=t2s[s3_][:, 0:JB], i=swap,
                                   sb_=sinb: e.scalar_tensor_tensor(
                                       o, i, 1.0, sb_, ALU.bypass, ALU.mult),
                                   inc="ve", waits=[slot_rel["t2"][s3_]])
                    bank_rel[bs + pi] = ev2
                    eva = P.op("ve", lambda e, o=qkr[s2][:, 0:JB],
                               a=t1s[s3_][:, 0:JB], b=t2s[s3_][:, 0:JB]:
                               e.tensor_add(o, a, b),
                               inc="ve", waits=[ev2, slot_rel["qkr"][s2],
                                                sq_slot[s2]])
                    slot_rel["t1"][s3_] = eva
                    slot_rel["t2"][s3_] = eva
                    evsq_ = None
                    for c4 in range(JBH):
                        col = pi * (HH * TT) + (jb * JBH + c4) * TT + t
                        evsq_ = P.op("sc", lambda e, i=qkr[s2][:, c4 * 128:
                                                               (c4 + 1) * 128],
                                     a=sqstrip[:, col:col + 1]:
                                     e.activation(sqtrash, i, AF.Square,
                                                  accum_out=a),
                                     inc="sc", waits=[eva])
                    sq_slot[s2] = evsq_
                    if "notr" in ablate:
                        slot_rel["qkr"][s2] = eva
                        continue

                    def _mk_tr(s2=s2, eva=eva, t=t, jb=jb, dstD=dstD):
                        ph = tr_ct[0] % 2
                        tr_ct[0] += 1
                        P.waits("te", [eva, ev_idn, pst_rel[ph]])
                        evtr = None
                        for sub in range(JBH):
                            evtr = P.op("te", lambda e,
                                        o=PSTH[ph][:, sub * 128:(sub + 1) * 128],
                                        i=qkr[s2][:, sub * 128:(sub + 1) * 128]:
                                        e.transpose(o, i, idn_bf[:, :]),
                                        inc="pe" if sub == JBH - 1 else None)
                        slot_rel["qkr"][s2] = evtr
                        evcp = P.op("sc", lambda e, o=trev[s2][:, 0:JB],
                                    i=PSTH[ph][:, 0:JB]: e.copy(o, i),
                                    inc="sc", waits=[evtr, trv_sem[s2]])
                        pst_rel[ph] = evcp
                        dst = dstD[jb * JB:(jb + 1) * JB,
                                   t * 128:(t + 1) * 128].rearrange(
                                       "(s p) t -> p s t", p=128)
                        evd = P.dma("sc", dst, trev[s2][:, 0:JB].rearrange(
                            "p (s u) -> p s u", u=128), f"tr{s2}", waits=[evcp])
                        trv_sem[s2] = evd
                    pend_tr.append(_mk_tr)
                    if len(pend_tr) > 2:
                        pend_tr.pop(0)()
                s5 = vev_ct % 2
                vev_ct += 1
                evvc = P.op("sc", lambda e, o=vev[s5][:, 0:JB], i=vb[:, 0:JB]:
                            e.copy(o, i), inc="sc", waits=[evv, vev_sem[s5]])
                bank_rel[bs + 2] = evvc
                vdst = Vs.rearrange("p (h kt j) -> p h kt j", h=HH, j=128)[
                    :, jb * JBH:(jb + 1) * JBH, t, :]
                evvd = P.dma("sc", vdst, vev[s5][:, 0:JB].rearrange(
                    "p (h j) -> p h j", j=128), f"vv{s5}", waits=[evvc])
                vev_sem[s5] = evvd
            slot_rel["xbf"][sl] = ("pe", P.cnt["pe"])
            slot_rel["xst"][sl] = ("ve", P.cnt["ve"])
        for fn_ in pend_tr:
            fn_()
        pend_tr.clear()

        phaseA_loads_done = [P.final("tr0"), P.final("tr1"), P.final("tr2"),
                             P.final("vv0"), P.final("vv1")]
        phaseA_loads_done = [e for e in phaseA_loads_done if e[1] > 0]
        barA = P.bar()
        pp_state["bar0"] = barA

        evmx = []
        for pi2 in range(2):
            for h2 in range(HH):
                c0_ = (pi2 * HH + h2) * TT
                evmx.append(P.op("ve", lambda e,
                                 o=nqk_sb[:, pi2 * HH + h2:pi2 * HH + h2 + 1],
                                 i=sqstrip[:, c0_:c0_ + TT]:
                                 e.reduce_max(o, i, mybir.AxisListType.X),
                                 inc="ve", waits=barA))
        evnd = P.dma("sy", NQKD[:, :], nqk_sb[:, :], "nq", waits=evmx)
        NQT = NQKD.rearrange("p c -> c p")
        evlq = P.op("sy", lambda e: e.dma_start(
            out=mqh[0:8, :], in_=NQT[0:8, :],
            allow_slow_non_contiguous=True), inc="nq", n=16, waits=[evnd])
        evlk = P.op("sy", lambda e: e.dma_start(
            out=mkh[0:8, :], in_=NQT[8:16, :],
            allow_slow_non_contiguous=True), inc="nq", n=16)
        evr1 = P.op("ve", lambda e: e.reduce_max(mq1[0:8, 0:1], mqh[0:8, :],
                                                 mybir.AxisListType.X),
                    inc="ve", waits=[evlk])
        evr2_ = P.op("ve", lambda e: e.reduce_max(mk1[0:8, 0:1], mkh[0:8, :],
                                                  mybir.AxisListType.X),
                    inc="ve", waits=[evlk])
        evr3 = P.op("ve", lambda e: e.tensor_mul(mq1[0:8, 0:1], mq1[0:8, 0:1],
                                                 mk1[0:8, 0:1]),
                    inc="ve", waits=[evr1, evr2_])
        evr4 = P.op("sc", lambda e: e.activation(mk1[0:8, 0:1], mq1[0:8, 0:1],
                                                 AF.Ln, scale=1.0 / 128.0),
                    inc="sc", waits=[evr3])
        evr5 = P.op("sc", lambda e: e.activation(mq1[0:8, 0:1], mk1[0:8, 0:1],
                                                 AF.Exp, scale=0.5),
                    inc="sc", waits=[evr4])
        evr6 = P.op("ve", lambda e: e.tensor_scalar(mq1[0:8, 0:1], mq1[0:8, 0:1],
                                                    -1.0, None, ALU.mult),
                    inc="ve", waits=[evr5])
        evst_ = P.dma("sy", CND.rearrange("x h -> h x"), mq1[0:8, 0:1], "nq",
                      waits=[evr6])
        negc_ready = P.dma("sy", negc[:, 0:HH], CND.broadcast_to((128, HH)),
                           "nq", waits=[evst_])
        if stop_after == 'A':
            P.waits("sy", phaseA_loads_done + barA +
                    [P.final("pq0"), P.final("pq1")])
            P.dma("sy", outT[0:128, 0:128], cst_f[:, :].bitcast(F32) if False else cst_f[:, :], "out")
            P.wait("sy", P.final("out"))
            return _emit(nc, P, es)

        cb2 = Carve(arb)
        o_sb = cb2.take(HH * S, BF16)
        cc2 = Carve(arc)
        qh = [cc2.take(S, BF16) for _ in range(2)]
        kh = [cc2.take(S, BF16) for _ in range(2)]
        vh = [cc2.take(S, BF16) for _ in range(2)]
        sqs = cc2.take(S, BF16)
        cd2 = Carve(ard)
        ptp = [cd2.take(1024, BF16) for _ in range(3)]
        bc_sb = [cd2.take(512, F32) for _ in range(2)]

        qkv_rel = [None, None]
        P.waits("sy", phaseA_loads_done + barA)

        SPAIR = [PSALL[:, 0:1024], PSALL[:, 1024:2048]]
        OB = [PS[4], PS[5]]
        RBF = [PSR[:, 0:512], PST_.bitcast(F32)[:, 0:512]]
        RB = [RBF[0][0:1, :], RBF[1][0:1, :]]
        o_evac_ev = None
        ptp_rel = [None, None, None]
        psb_exp = [None, None]
        ob_rel = [None, None]
        rb_rel = [None, None]
        rrowd_rel = [None, None]
        rrow_war = [None, None]
        bc_rel = [None, None]
        pend_epi = []
        gctl = [0]

        Vs5 = Vs.rearrange("p (h kt j) -> p h kt j", h=HH, j=128)

        load_evs = {}

        def emit_loads(h):
            sl = h % 2
            evq = P.dma("sy", qh[sl], Qs[h * 128:(h + 1) * 128, :], f"qh{sl}",
                        waits=[qkv_rel[sl]] + ([negc_ready] if h == 1 else []))
            evk = P.dma("sy", kh[sl], Ks[h * 128:(h + 1) * 128, :], f"qh{sl}")
            evv = P.dma("sy", vh[sl].rearrange("p (kt j) -> p kt j", j=128),
                        Vs5[:, h, :, :], f"qh{sl}")
            load_evs[h] = evv

        emit_loads(0)
        for h in range(HH):
            sl = h % 2
            evv = load_evs[h]
            negc_ev = negc_ready

            for g in range(NG):
                t0 = 4 * g
                kmax = t0 + 3
                P2 = (kmax + 1) // 2
                ob = gctl[0] % 2
                gctl[0] += 1

                def _emit_pair_S(p, first):
                    slot = p % 2
                    ev = None
                    for half in (0, 1):
                        K = 2 * p + half
                        qlo = max(0, K - t0) * 128
                        diag = K >= t0
                        w0 = [psb_exp[slot]]
                        if first and half == 0:
                            w0 += [ob_rel[ob], rb_rel[ob], ev_maska]
                            if t0 == 0:
                                w0 += [evv]
                        ev = P.op("te", lambda e,
                                  o=SPAIR[slot][:, half * 512 + qlo:
                                                half * 512 + 512],
                                  l=kh[sl][:, K * 128:(K + 1) * 128],
                                  r=qh[sl][:, g * 512 + qlo:(g + 1) * 512],
                                  d=diag:
                                  e.matmul(o, l, r, start=True, stop=not d),
                                  inc=None if diag else
                                      ("pe" if half == 1 else None), waits=w0)
                        if diag:
                            ev = P.op("te", lambda e,
                                      o=SPAIR[slot][:, half * 512 + qlo:
                                                    half * 512 + qlo + 128]:
                                      e.matmul(o, maska_bf[:, :], idn_bf[:, :],
                                               start=False, stop=True),
                                      inc="pe" if half == 1 else None)
                    return ev

                def _consume(p):
                    slot = p % 3
                    evPV = None
                    for half in (0, 1):
                        K = 2 * p + half
                        qlo = max(0, K - t0) * 128
                        pslice = ptp[slot][:, half * 512 + qlo:half * 512 + 512]
                        P.waits("te", [exp_evs[p]])
                        P.op("te", lambda e, r=pslice, a=(K == 0),
                             b=(K == kmax), ql=qlo, rb_=RB[ob]:
                             e.matmul(rb_[0:1, ql:512], ones_bf[:, 0:1], r,
                                      start=a, stop=b))
                        evPV = P.op("te", lambda e, o=OB[ob][:, qlo:512],
                                    l=vh[sl][:, K * 128:(K + 1) * 128],
                                    r=pslice, a=(K == 0), b=(K == kmax):
                                    e.matmul(o, l, r, start=a, stop=b),
                                    inc="pe")
                    ptp_rel[slot] = evPV
                    return evPV

                exp_evs = {}
                evS_next = _emit_pair_S(0, True)
                evpv_stop = None
                for p in range(P2):
                    s2_ = p % 2
                    s3_ = p % 3
                    evS = evS_next
                    if p < P2 - 1:
                        evS_next = _emit_pair_S(p + 1, False)
                    if p >= 1:
                        _consume(p - 1)
                    qlo0 = max(0, 2 * p - t0) * 128
                    evE = P.op("sc", lambda e, o=ptp[s3_][:, qlo0:1024],
                               i=SPAIR[s2_][:, qlo0:1024], b=negc[:, h:h + 1]:
                               e.activation(o, i, AF.Exp, bias=b, scale=RSQRT_DK),
                               inc="sc", waits=[evS, negc_ev, ptp_rel[s3_]])
                    exp_evs[p] = evE
                    psb_exp[s2_] = evE
                evpv_stop = _consume(P2 - 1)

                evl2 = P.op("sc", lambda e, ob_=ob: e.activation(
                    rrowB[ob_][0:1, :], RB[ob_][0:1, 0:512], AF.Ln),
                            inc="sc", waits=[evpv_stop, rrow_war[ob]])
                rb_rel[ob] = evl2
                evr2 = P.op("sc", lambda e, ob_=ob: e.activation(
                    rrowB[ob_][0:1, :], rrowB[ob_][0:1, :], AF.Exp, scale=-1.0),
                            inc="sc", waits=[evl2])
                evd1 = P.dma("sc", RROWD[ob:ob + 1, :], rrowB[ob][0:1, :],
                             f"rb{ob}", waits=[evr2, rrowd_rel[ob]])
                rrow_war[ob] = evd1
                evd2 = P.dma("sc", bc_sb[ob][:, :],
                             RROWD[ob:ob + 1, :].broadcast_to((128, 512)),
                             f"rc{ob}", waits=[evd1, bc_rel[ob]])
                rrowd_rel[ob] = evd2
                oe = P.op("ve", lambda e, o=o_sb[:, h * S + g * 512:
                                                 h * S + (g + 1) * 512],
                          ob_=ob: e.scalar_tensor_tensor(
                              o, OB[ob_][:, 0:512], 1.0, bc_sb[ob_][:, :],
                              ALU.bypass, ALU.mult),
                          inc="ve", waits=[evpv_stop, evd2])
                ob_rel[ob] = oe
                bc_rel[ob] = oe
                o_evac_ev = oe
                if g == 0 and h + 1 < HH:
                    emit_loads(h + 1)
            qkv_rel[sl] = evpv_stop
            if "nopp" not in ablate:
                pump_prepass(12 if HH >= 8 else 96)
        o_evac_ev = P.final("ve")
        psr_rel = rb_rel[0]
        ps2_rel = ("sc", P.cnt["sc"])
        if "nopp" not in ablate:
            pump_prepass(len(pp_chunks))

        barB = P.bar()
        if stop_after == 'B':
            P.waits("sy", barB)
            P.dma("sy", outT[0:128, 0:128], cst_f[:, :], "out")
            P.wait("sy", P.final("out"))
            return _emit(nc, P, es)

        cc3 = Carve(arc)
        wo_sb = cc3.take(HH * D, BF16)
        wo_st = cc3.take(HH * D // 2, F32)
        cd3 = Carve(ard)
        yst = [cd3.take(512, BF16) for _ in range(2)]

        ev_wo = None
        nhalf = max(HH // 2, 1)
        ev_free = None
        for hf in range(2 if HH >= 2 else 1):
            cs = slice(hf * nhalf, (hf + 1) * nhalf)
            evl = P.dma("sy", wo_st.rearrange("p (c j) -> p c j", j=D)[:, 0:nhalf, :],
                        woT.rearrange("(c p) j -> p c j", p=128)[:, cs, :],
                        "wld", waits=barB + [ev_free, P.final("pq0"),
                                             P.final("pq1")])
            ev_wo = P.op("ve", lambda e, c=cs: e.tensor_copy(
                wo_sb.rearrange("p (c j) -> p c j", j=D)[:, c, :],
                wo_st.rearrange("p (c j) -> p c j", j=D)[:, 0:nhalf, :]),
                inc="ve", waits=[evl])
            ev_free = ev_wo

        ybank_rel = [None, None]
        yst_sem = [None, None]
        yct = 0
        evcc = [None, None, None, None]
        groups = [[i, i + 1] for i in range(0, NC, 2)]
        TQ = TT // 4
        for tt in range(TT):
            for j4 in range(ND4):
                yb = yct % 2
                ps = PS[yb]
                P.waits("te", [ev_wo, o_evac_ev, ybank_rel[yb]])
                evy = None
                for dc in range(HH):
                    lhs = o_sb[:, dc * S + tt * 128:dc * S + (tt + 1) * 128]
                    rhs = wo_sb.rearrange("p (c j) -> p c j", j=D)[
                        :, dc, j4 * 512:(j4 + 1) * 512]
                    evy = P.op("te", lambda e, o=ps[:, 0:512], l=lhs, r=rhs,
                               a=(dc == 0), b=(dc == HH - 1):
                               e.matmul(o, l, r, start=a, stop=b),
                               inc="pe" if dc == HH - 1 else None)
                evcp = P.op("sc", lambda e, o=yst[yb][:, :], i=ps[:, 0:512]:
                            e.copy(o, i), inc="sc", waits=[evy, yst_sem[yb]])
                ybank_rel[yb] = evcp
                evd = P.dma("sc", YP[tt * 128:(tt + 1) * 128,
                                     j4 * 512:(j4 + 1) * 512],
                            yst[yb][:, :], f"yp{yb}", waits=[evcp])
                yst_sem[yb] = evd
                yct += 1
            if (tt + 1) % TQ == 0:
                q = tt // TQ
                ypq = YP[q * (S // 4):(q + 1) * (S // 4), :]
                ysq = YSH[q * (S // 8):(q + 1) * (S // 8), :]
                evcc[q] = P.op("gp", lambda e, a=ypq, b=ysq:
                               e.collective_compute("ReduceScatter", ALU.add,
                                                    groups, [a.opt()], [b.opt()]),
                               inc=f"cc{q}", n=1,
                               waits=[P.final("yp0"), P.final("yp1")])

        if stop_after == 'W':
            P.waits("sy", [P.final("yp0"), P.final("yp1")])
            P.dma("sy", outT[0:128, 0:128], cst_f[:, :], "out")
            P.wait("sy", P.final("out"))
            return _emit(nc, P, es)
        if stop_after == 'R':
            for q in range(4):
                P.wait("gp", P.final(f"cc{q}"))
            P.dma("gp", outT[0:128, 0:128], cst_f[:, :], "out")
            P.wait("gp", P.final("out"))
            return _emit(nc, P, es)
        barC1 = P.bar()
        wo_end = ("pe", P.cnt["pe"])
        cb3 = Carve(arb)
        cb3.off = 81920
        h1t = [cb3.take(D, F32) for _ in range(2)]
        ccX = Carve(arc)
        x2t = [ccX.take(D, BF16) for _ in range(2)]
        cdT = Carve(ard)
        cdT.off = 33280
        mucol = cdT.take(2, F32)
        sqcol = cdT.take(2, F32)
        varcol = cdT.take(2, F32)
        rstdcol = cdT.take(2, F32)
        msqcol = cdT.take(2, F32)
        ccL = Carve(arc)
        ccL.off = HH * D * 2
        ysr = [ccL.take(D, BF16) for _ in range(2)]
        xhr = [ccL.take(D, F32) for _ in range(2)]
        cd4 = Carve(ard)
        cd4.off = 2 * 1024 + 128
        trev2 = [cd4.take(512, BF16) for _ in range(2)]

        ln_rel = [None, None]
        x2t_rel = [None, None]
        trv2_sem = [None, None]
        pst2_rel = [None, None]
        sq1_rel = None
        THQ = TH // 4
        for tt in range(TH):
            sl = tt % 2
            evy = P.dma("sy", ysr[sl], YSH[tt * 128:(tt + 1) * 128, :], f"ln{sl}",
                        waits=[evcc[tt // THQ], ln_rel[sl], ev_wo,
                               P.final("pq0"), P.final("pq1")])
            evxh = P.dma("sy", xhr[sl], xh[tt * 128:(tt + 1) * 128, :], f"ln{sl}")
            evh1 = P.op("ve", lambda e, o=h1t[sl], a=ysr[sl], b=xhr[sl]:
                        e.tensor_add(o, a, b), inc="ve",
                        waits=[evxh, P.final("pq0"), P.final("pq1")])
            evmu = P.op("ve", lambda e, o=mucol[:, 0:1], i=h1t[sl]:
                        e.tensor_reduce(o, i, mybir.AxisListType.X, ALU.add),
                        inc="ve", waits=[evh1])
            evsq = P.op("sc", lambda e, i=h1t[sl], o=ysr[sl]: e.activation(
                o, i, AF.Square, accum_out=sqcol[:, 0:1]),
                inc="sc", waits=[evh1, sq1_rel])
            evmu2 = P.op("ve", lambda e: e.tensor_scalar(
                mucol[:, 0:1], mucol[:, 0:1], 1.0 / D, None, ALU.mult),
                inc="ve", waits=[evmu])
            evms = P.op("sc", lambda e: e.activation(msqcol[:, 0:1], mucol[:, 0:1],
                                                     AF.Square),
                        inc="sc", waits=[evmu2])
            evvar = P.op("ve", lambda e: e.scalar_tensor_tensor(
                varcol[:, 0:1], sqcol[:, 0:1], 1.0 / D, msqcol[:, 0:1],
                ALU.mult, ALU.subtract), inc="ve", waits=[evsq, evms])
            evlnv = P.op("sc", lambda e: e.activation(rstdcol[:, 0:1],
                                                      varcol[:, 0:1], AF.Ln,
                                                      bias=epscol[:, 0:1]),
                         inc="sc", waits=[evvar])
            evrst = P.op("sc", lambda e: e.activation(rstdcol[:, 0:1],
                                                      rstdcol[:, 0:1], AF.Exp,
                                                      scale=-0.5),
                         inc="sc", waits=[evlnv])
            evx2 = P.op("ve", lambda e, o=x2t[sl], i=h1t[sl]: e.tensor_scalar(
                o, i, mucol[:, 0:1], rstdcol[:, 0:1], ALU.subtract, ALU.mult),
                inc="ve", waits=[evrst, x2t_rel[sl], wo_end])
            sq1_rel = evx2
            ln_rel[sl] = evx2
            for dg in range(ND4):
                ph = (tt * ND4 + dg) % 2
                P.waits("te", [evx2, pst2_rel[ph]])
                evtr = None
                for sub in range(4):
                    evtr = P.op("te", lambda e,
                                o=PSTH[ph][:, sub * 128:(sub + 1) * 128],
                                i=x2t[sl][:, dg * 512 + sub * 128:
                                          dg * 512 + (sub + 1) * 128]:
                                e.transpose(o, i, idn_bf[:, :]),
                                inc="pe" if sub == 3 else None)
                if dg == ND4 - 1:
                    x2t_rel[sl] = evtr
                evcp = P.op("sc", lambda e, o=trev2[ph][:, 0:512],
                            i=PSTH[ph][:, 0:512]: e.copy(o, i),
                            inc="sc", waits=[evtr, trv2_sem[ph]])
                pst2_rel[ph] = evcp
                dst = X2s.rearrange("p (q c u) -> p q c u", c=KC, u=512)[
                    :, tt // THQ, dg * 4:(dg + 1) * 4,
                    (tt % THQ) * 128:(tt % THQ) * 128 + 128]
                evd = P.dma("sc", dst, trev2[ph][:, 0:512].rearrange(
                    "p (s u) -> p s u", u=128), f"x2q{tt // THQ}_{ph}",
                    waits=[evcp])
                trv2_sem[ph] = evd

        x2_q_done = [[P.final(f"x2q{q}_0"), P.final(f"x2q{q}_1")]
                     for q in range(4)]
        barC2 = P.bar()
        if stop_after == 'C':
            P.waits("sy", [e for q in x2_q_done for e in q] + barC2)
            P.dma("sy", outT[0:128, 0:128], cst_f[:, :], "out")
            P.wait("sy", P.final("out"))
            return _emit(nc, P, es)

        cb4 = Carve(arb)
        fuse = cb4.take(ICB * 512, BF16)
        x2fA = cb4.take(KC * 512, BF16)
        cb4b = Carve(arb)
        cb4b.off = 81920
        x2fB = cb4b.take(KC * 512, BF16)
        x2fr = [x2fA, x2fB]
        cc4 = Carve(arc)
        zar = cc4.take(KC * 512, F32)
        wdsl = [cc4.take(ICB * 128, BF16) for _ in range(2)]
        cd5 = Carve(ard)
        cd5.off = 2 * 2048 + 128 + 2 * 1024 + 64
        wgsl = [cd5.take(KC * 128, BF16) for _ in range(2)]
        wusl = [cd5.take(KC * 128, BF16) for _ in range(2)]
        silu_sb = [cd5.take(512, BF16) for _ in range(2)]
        sqf = cd5.take(512, F32)
        bmu = cd5.take(512, F32)
        brs = cd5.take(512, F32)
        tmpn = cd5.take(512, F32)

        PSTF = PSALL[:, 7 * 512:8 * 512]
        WG4 = WGs.rearrange("p (b c u) -> p b c u", c=KC, u=128)
        WU4 = WUs.rearrange("p (b c u) -> p b c u", c=KC, u=128)
        WD4 = WDs.rearrange("p (b c u) -> p b c u", c=ICB, u=128)
        X2T4 = X2s.rearrange("p (q c u) -> p q c u", c=KC, u=512)
        out3 = outT.rearrange("(c p) q -> p c q", p=128)

        P.waits("sy", [P.final("pq0"), P.final("pq1")])
        x2f_sem = None
        fuse_rel = None
        x2f_war = [None, None]
        gsl_rel = [None, None]
        usl_rel = [None, None]
        silu_rel = [None, None]
        wdsl_rel = [None, None]
        gb_rel = [None, None]
        ub_rel = [None, None]
        zb_rel = [None, None]
        st_rel = None
        bcn_rel = None
        out_sem = None
        for qg in range(QG):
            x2f = x2fr[qg % 2]
            qw = THQ * 128
            q_lo = (qg * 512) // qw
            q_hi = ((qg + 1) * 512 - 1) // qw
            need_q = [e for q_ in range(q_lo, q_hi + 1) for e in x2_q_done[q_]]
            evxf = P.dma("sy", x2f.rearrange("p (c u) -> p c u", u=512),
                         X2T4[:, qg, :, :], "x2l",
                         waits=need_q + ([wo_end, P.final("pq0"),
                                          P.final("pq1")] if qg <= 1 else []) +
                               (x2_q_done[3] if qg == 1 else []) +
                               (x2f_war[qg % 2] or []))
            for icb in range(ICB):
                sl = icb % 2
                evg = P.dma("sy", wgsl[sl].rearrange("p (c u) -> p c u", u=128),
                            WG4[:, icb, :, :], f"fg{sl}",
                            waits=[gsl_rel[sl]])
                evu = P.dma("sy", wusl[sl].rearrange("p (c u) -> p c u", u=128),
                            WU4[:, icb, :, :], f"fu{sl}",
                            waits=[usl_rel[sl]])
                gb = PS[sl]
                ub = PS[2 + sl]
                P.waits("te", [evg, evu, evxf, gb_rel[sl], ub_rel[sl], fuse_rel])
                evgm = evum = None
                for kc in range(KC):
                    rhs = x2f.rearrange("p (c u) -> p c u", u=512)[:, kc, :]
                    evgm = P.op("te", lambda e, o=gb[:, 0:512],
                                l=wgsl[sl].rearrange("p (c u) -> p c u",
                                                     u=128)[:, kc, :], r=rhs,
                                a=(kc == 0), b=(kc == KC - 1):
                                e.matmul(o, l, r, start=a, stop=b),
                                inc="pe" if kc == KC - 1 else None)
                for kc in range(KC):
                    rhs = x2f.rearrange("p (c u) -> p c u", u=512)[:, kc, :]
                    evum = P.op("te", lambda e, o=ub[:, 0:512],
                                l=wusl[sl].rearrange("p (c u) -> p c u",
                                                     u=128)[:, kc, :], r=rhs,
                                a=(kc == 0), b=(kc == KC - 1):
                                e.matmul(o, l, r, start=a, stop=b),
                                inc="pe" if kc == KC - 1 else None)
                gsl_rel[sl] = evgm
                usl_rel[sl] = evum
                evsi = P.op("sc", lambda e, o=silu_sb[sl], i=gb[:, 0:512],
                            s_=sgc[:, icb:icb + 1]: e.activation(
                                o, i, AF.Silu, scale=s_),
                            inc="sc", waits=[evgm, ev_sg, silu_rel[sl]])
                gb_rel[sl] = evsi
                evfu = P.op("ve", lambda e, o=fuse[:, icb * 512:(icb + 1) * 512],
                            i=ub[:, 0:512], s_=suc[:, icb:icb + 1],
                            si=silu_sb[sl]: e.scalar_tensor_tensor(
                                o, i, s_, si, ALU.mult, ALU.mult),
                            inc="ve", waits=[evum, ev_su, evsi])
                ub_rel[sl] = evfu
                silu_rel[sl] = evfu
            fuse_done = P.final("ve")

            for dcb in range(KC):
                sl = dcb % 2
                evw = P.dma("sy", wdsl[sl].rearrange("p (c u) -> p c u", u=128),
                            WD4[:, dcb, :, :], f"fd{sl}",
                            waits=[wdsl_rel[sl]])
                zb = PS[sl]
                P.waits("te", [evw, ("ve", fuse_done[1]), zb_rel[sl],
                               gb_rel[sl], ub_rel[sl]])
                evz = None
                for ic in range(ICB):
                    evz = P.op("te", lambda e, o=zb[:, 0:512],
                               l=wdsl[sl].rearrange("p (c u) -> p c u",
                                                    u=128)[:, ic, :],
                               r=fuse[:, ic * 512:(ic + 1) * 512],
                               a=(ic == 0), b=(ic == ICB - 1):
                               e.matmul(o, l, r, start=a, stop=b),
                               inc="pe" if ic == ICB - 1 else None)
                wdsl_rel[sl] = evz
                evzs = P.op("ve", lambda e, o=zar.rearrange(
                    "p (c u) -> p c u", u=512)[:, dcb, :], i=zb[:, 0:512],
                    s_=sdc[:, dcb:dcb + 1],
                    x_=x2f.rearrange("p (c u) -> p c u", u=512)[:, dcb, :]:
                    e.scalar_tensor_tensor(o, i, s_, x_, ALU.mult, ALU.add),
                    inc="ve", waits=[evz, ev_sd, out_sem])
                zb_rel[sl] = evzs
                evq2 = P.op("sc", lambda e, i=zar.rearrange(
                    "p (c u) -> p c u", u=512)[:, dcb, :]: e.activation(
                        sqf, i, AF.Square), inc="sc", waits=[evzs, st_rel])
                evm1 = P.op("te", lambda e, i=zar.rearrange(
                    "p (c u) -> p c u", u=512)[:, dcb, :], a=(dcb == 0),
                    b=(dcb == KC - 1): e.matmul(
                        PSR[0:1, 0:512], onescol_f[:, 0:1], i, start=a, stop=b),
                    inc=None, waits=[evzs, (psr_rel if dcb == 0 else None)] +
                                    (x2_q_done[3] if qg == 0 and
                                     dcb == 0 else []))
                evm2 = P.op("te", lambda e, a=(dcb == 0), b=(dcb == KC - 1):
                            e.matmul(PSTF[0:1, 0:512], onescol_f[:, 0:1], sqf,
                                     start=a, stop=b),
                            inc="pe", waits=[evq2,
                                             (ps2_rel if dcb == 0 else None)] +
                                            (x2_q_done[3] if qg == 0 and
                                             dcb == 0 else []))
                st_rel = evm2
            evmr = P.op("ve", lambda e: e.tensor_scalar(
                murow[0:1, :], PSR[0:1, 0:512], 1.0 / D, None, ALU.mult),
                inc="ve", waits=[st_rel])
            psr_rel = evmr
            evms2 = P.op("sc", lambda e: e.activation(msq[0:1, :], murow[0:1, :],
                                                      AF.Square),
                         inc="sc", waits=[evmr])
            evvr = P.op("ve", lambda e: e.scalar_tensor_tensor(
                varow[0:1, :], PSTF[0:1, 0:512], 1.0 / D, msq[0:1, :],
                ALU.mult, ALU.subtract), inc="ve", waits=[evms2, st_rel])
            ps2_rel = evvr
            evlv = P.op("sc", lambda e: e.activation(varow[0:1, :], varow[0:1, :],
                                                     AF.Ln, bias=epscol[0:1, 0:1]),
                        inc="sc", waits=[evvr])
            evrs = P.op("sc", lambda e: e.activation(varow[0:1, :], varow[0:1, :],
                                                     AF.Exp, scale=-0.5),
                        inc="sc", waits=[evlv])
            evb1 = P.op("te", lambda e: e.matmul(PS[4][:, 0:512], onesrow_f[0:1, :],
                                                 murow[0:1, :], start=True,
                                                 stop=True),
                        inc="pe", waits=[evmr, bcn_rel] +
                                        ([o_evac_ev] if qg == 0 else []))
            evb2 = P.op("te", lambda e: e.matmul(PS[5][:, 0:512], onesrow_f[0:1, :],
                                                 varow[0:1, :], start=True,
                                                 stop=True),
                        inc="pe", waits=[evrs])
            evc1 = P.op("ve", lambda e: e.tensor_copy(bmu, PS[4][:, 0:512]),
                        inc="ve", waits=[evb1])
            evc2_ = P.op("ve", lambda e: e.tensor_copy(brs, PS[5][:, 0:512]),
                         inc="ve", waits=[evb2])
            bcn_rel = evc2_
            evn = None
            for dcb in range(KC):
                zch = zar.rearrange("p (c u) -> p c u", u=512)[:, dcb, :]
                evt = P.op("ve", lambda e, i=zch: e.tensor_sub(tmpn, i, bmu),
                           inc="ve", waits=[evc1, evc2_, evn])
                evn = P.op("ve", lambda e, o=zch: e.tensor_mul(o, tmpn, brs),
                           inc="ve", waits=[evt])
            out_sem = P.dma("sc", out3[:, :, qg * 512:(qg + 1) * 512],
                            zar.rearrange("p (c u) -> p c u", u=512), "out",
                            waits=[evn])
            fuse_rel = ("pe", P.cnt["pe"])
            x2f_war[qg % 2] = [fuse_rel, ("ve", P.cnt["ve"])]
            gb_rel[0] = zb_rel[0]
            gb_rel[1] = zb_rel[1]

        P.wait("sy", P.final("out"))
        P.wait("gp", P.final("out"))
        return _emit(nc, P, es)


def _emit(nc, P, es):
    sems = {}
    for name in P.cnt:
        sems[name] = es.enter_context(nc.semaphore(name))
    block = es.enter_context(nc.Block())

    def emit(eng_obj, lst):
        for item in lst:
            if item[0] == "w":
                _, name, v = item
                eng_obj.wait_ge(sems[name], v)
            else:
                _, fn, incn, n = item
                ins = fn(eng_obj)
                if incn is not None:
                    ins.then_inc(sems[incn], n)

    @block.sync
    def _(e):
        emit(e, P.ops["sy"])

    @block.tensor
    def _(e):
        emit(e, P.ops["te"])

    @block.vector
    def _(e):
        emit(e, P.ops["ve"])

    @block.scalar
    def _(e):
        emit(e, P.ops["sc"])

    @block.gpsimd
    def _(e):
        emit(e, P.ops["gp"])
    return nc


def kernel(L0_attn_x, L0_attn_wq, L0_attn_wk, L0_attn_wv, L0_attn_wo,
           L0_ffn_w_gate, L0_ffn_w_up, L0_ffn_w_down,
           L0_ffn_s_gate, L0_ffn_s_up, L0_ffn_s_down):
    cfg = CFG_FULL
    x = np.asarray(L0_attn_x)
    in_maps = make_host_inputs(
        cfg, x, np.asarray(L0_attn_wq), np.asarray(L0_attn_wk),
        np.asarray(L0_attn_wv), np.asarray(L0_attn_wo),
        np.asarray(L0_ffn_w_gate), np.asarray(L0_ffn_w_up),
        np.asarray(L0_ffn_w_down), np.asarray(L0_ffn_s_gate),
        np.asarray(L0_ffn_s_up), np.asarray(L0_ffn_s_down))
    nc = build_program(cfg)
    res = run_bass_kernel_spmd(nc, in_maps, core_ids=list(range(cfg["NCORES"])))
    return assemble_output(cfg, res.results)


# version 21
# speedup vs baseline: 1.0892x; 1.0892x over previous
import numpy as np
import ml_dtypes
from contextlib import ExitStack

import concourse.bass as bass
import concourse.mybir as mybir
from concourse.bass_utils import run_bass_kernel_spmd

F32 = mybir.dt.float32
BF16 = mybir.dt.bfloat16
I8 = mybir.dt.int8
AF = mybir.ActivationFunctionType
ALU = mybir.AluOpType

CFG_FULL = dict(D=2048, S=4096, INTER=8192, NCORES=8)

RSQRT_DK = 1.0 / float(np.sqrt(128.0))

ENGS = ("sy", "te", "ve", "sc", "gp")


class Prog:

    def __init__(self):
        self.ops = {e: [] for e in ENGS}
        self.cnt = {}
        self.seen = {e: {} for e in ENGS}

    def sem(self, name):
        if name not in self.cnt:
            self.cnt[name] = 0

    def wait(self, eng, ev):
        if ev is None:
            return
        name, v = ev
        self.sem(name)
        if self.seen[eng].get(name, -1) >= v:
            return
        self.seen[eng][name] = v
        self.ops[eng].append(("w", name, v))

    def waits(self, eng, evs):
        for ev in evs:
            self.wait(eng, ev)

    def op(self, eng, fn, inc=None, n=1, waits=()):
        self.waits(eng, waits)
        ev = None
        if inc is not None:
            self.sem(inc)
            self.cnt[inc] += n
            ev = (inc, self.cnt[inc])
        self.ops[eng].append(("i", fn, inc, n))
        return ev

    def dma(self, eng, out, in_, sem, waits=()):
        return self.op(eng, (lambda e, o=out, i=in_: e.dma_start(out=o, in_=i)),
                       inc=sem, n=16, waits=waits)

    def final(self, name):
        self.sem(name)
        return (name, self.cnt[name])

    def bar(self):
        return [(s, self.cnt[s]) for s in ("pe", "ve", "sc", "gp")
                if self.cnt.get(s, 0) > 0]


def own_rows(cfg, hh):
    S = cfg["S"]
    rows = []
    for q in range(4):
        base = q * (S // 4) + hh * (S // 8)
        rows.extend(range(base, base + S // 8))
    return np.array(rows)


def make_host_inputs(cfg, x, wq, wk, wv, wo, wg, wu, wd, sg, su, sd):
    D, S, INTER, NC = cfg["D"], cfg["S"], cfg["INTER"], cfg["NCORES"]
    Dh = D // 2
    dk = 128
    half = dk // 2
    SH = S // 2

    inv_freq = (10000.0) ** (-2.0 * np.arange(half, dtype=np.float32) / dk)
    pos = np.arange(S, dtype=np.float32)
    ang = pos[:, None] * inv_freq[None, :]
    c = np.cos(ang).astype(np.float32)
    s = np.sin(ang).astype(np.float32)
    cos_tm = np.empty((S, dk), np.float32)
    cos_tm[:, 0::2] = c
    cos_tm[:, 1::2] = c
    sin_sgn = np.empty((S, dk), np.float32)
    sin_sgn[:, 0::2] = -s
    sin_sgn[:, 1::2] = s

    kk, qq = np.meshgrid(np.arange(128), np.arange(128), indexing="ij")
    maskT = (kk <= qq).astype(np.float32)
    maskadd = np.where(kk > qq, -1.0e30, 0.0).astype(np.float32).T.copy()
    idn = np.eye(128, dtype=np.float32)

    sg2d = np.ascontiguousarray(sg.reshape(INTER // 128, 128).T).astype(np.float32)
    su2d = np.ascontiguousarray(su.reshape(INTER // 128, 128).T).astype(np.float32)
    sd2d = np.ascontiguousarray(sd.reshape(D // 128, 128).T).astype(np.float32)

    wqT = wq.T
    wkT = wk.T
    wvT = wv.T
    woT = wo.T

    in_maps = []
    for core in range(NC):
        b = core // 2
        hh = core % 2
        jsl = slice(hh * Dh, (hh + 1) * Dh)
        in_maps.append({
            "xT": np.ascontiguousarray(x[b].T),
            "xh": np.ascontiguousarray(x[b][own_rows(cfg, hh), :]),
            "wqT": np.ascontiguousarray(wqT[:, jsl]),
            "wkT": np.ascontiguousarray(wkT[:, jsl]),
            "wvT": np.ascontiguousarray(wvT[:, jsl]),
            "woT": np.ascontiguousarray(woT[jsl, :]),
            "wg": wg, "wu": wu, "wd": wd,
            "sg2d": sg2d, "su2d": su2d, "sd2d": sd2d,
            "cos_tm": cos_tm, "sin_sgn": sin_sgn,
            "maskT": maskT, "maskadd": maskadd, "idn": idn,
        })
    return in_maps


def assemble_output(cfg, results):
    D, S, NC = cfg["D"], cfg["S"], cfg["NCORES"]
    SH = S // 2
    out = np.empty((NC // 2, S, D), np.float32)
    for core in range(NC):
        b = core // 2
        hh = core % 2
        out[b, own_rows(cfg, hh), :] = results[core]["outT"].T
    return out


def build_program(cfg, stop_after='D', ablate=()):
    D, S, INTER, NC = cfg["D"], cfg["S"], cfg["INTER"], cfg["NCORES"]
    Dh = D // 2
    HH = Dh // 128
    KC = D // 128
    TT = S // 128
    TGS = S // 512
    JB = min(512, Dh)
    NJB = Dh // JB
    JBH = JB // 128
    NG = TT // 4
    SH = S // 2
    TH = SH // 128
    QG = SH // 512
    ND4 = D // 512
    ICB = INTER // 128
    PPW = 256
    PPD = 64

    nc = bass.Bass("TRN2", target_bir_lowering=False)

    xT = nc.dram_tensor("xT", [D, S], F32, kind="ExternalInput").ap()
    xh = nc.dram_tensor("xh", [SH, D], F32, kind="ExternalInput").ap()
    wqT = nc.dram_tensor("wqT", [D, Dh], F32, kind="ExternalInput").ap()
    wkT = nc.dram_tensor("wkT", [D, Dh], F32, kind="ExternalInput").ap()
    wvT = nc.dram_tensor("wvT", [D, Dh], F32, kind="ExternalInput").ap()
    woT = nc.dram_tensor("woT", [Dh, D], F32, kind="ExternalInput").ap()
    wg = nc.dram_tensor("wg", [D, INTER], I8, kind="ExternalInput").ap()
    wu = nc.dram_tensor("wu", [D, INTER], I8, kind="ExternalInput").ap()
    wd = nc.dram_tensor("wd", [INTER, D], I8, kind="ExternalInput").ap()
    sg2d = nc.dram_tensor("sg2d", [128, INTER // 128], F32, kind="ExternalInput").ap()
    su2d = nc.dram_tensor("su2d", [128, INTER // 128], F32, kind="ExternalInput").ap()
    sd2d = nc.dram_tensor("sd2d", [128, D // 128], F32, kind="ExternalInput").ap()
    cos_tm = nc.dram_tensor("cos_tm", [S, 128], F32, kind="ExternalInput").ap()
    sin_sgn = nc.dram_tensor("sin_sgn", [S, 128], F32, kind="ExternalInput").ap()
    maskT = nc.dram_tensor("maskT", [128, 128], F32, kind="ExternalInput").ap()
    maskadd = nc.dram_tensor("maskadd", [128, 128], F32,
                             kind="ExternalInput").ap()
    idn = nc.dram_tensor("idn", [128, 128], F32, kind="ExternalInput").ap()
    outT = nc.dram_tensor("outT", [D, SH], F32, kind="ExternalOutput").ap()

    Qs = nc.dram_tensor("Qs", [Dh, S], BF16, kind="Internal").ap()
    Ks = nc.dram_tensor("Ks", [Dh, S], BF16, kind="Internal").ap()
    Vs = nc.dram_tensor("Vs", [128, HH * TT * 128], BF16, kind="Internal").ap()
    WGs = nc.dram_tensor("WGs", [128, ICB * KC * 128], BF16,
                         kind="Internal").ap()
    WUs = nc.dram_tensor("WUs", [128, ICB * KC * 128], BF16,
                         kind="Internal").ap()
    WDs = nc.dram_tensor("WDs", [128, KC * ICB * 128], BF16,
                         kind="Internal").ap()
    X2s = nc.dram_tensor("X2s", [128, (SH // 512) * KC * 512], BF16,
                         kind="Internal").ap()
    YP = nc.dram_tensor("YP", [S, D], BF16, kind="Internal").ap()
    RROWD = nc.dram_tensor("RROWD", [2, 512], F32, kind="Internal").ap()
    NQKD = nc.dram_tensor("NQKD", [128, 16], F32, kind="Internal").ap()
    CND = nc.dram_tensor("CND", [1, 8], F32, kind="Internal").ap()
    WOB = nc.dram_tensor("WOB", [128, HH * D], BF16, kind="Internal").ap()
    YSH = nc.dram_tensor("YSH", [SH, D], BF16, kind="Internal").ap()

    P = Prog()
    for s in ("pe", "ve", "sc", "gp"):
        P.sem(s)
    es = ExitStack()
    SBt = lambda n, sh, dt: es.enter_context(nc.sbuf_tensor(n, sh, dt)).ap()
    PSB = lambda n, sh, dt: es.enter_context(nc.psum_tensor(n, sh, dt)).ap()

    with es:
        arb = SBt("arb", [128, 98304], I8)
        arc = SBt("arc", [128, 65536], I8)
        ard = SBt("ard", [128, 35840], I8)
        ones_bf = SBt("ones_bf", [128, 2], BF16)
        negones_bf = SBt("negones_bf", [1, 128], BF16)
        onescol_f = SBt("onescol_f", [128, 2], F32)
        onesrow_f = SBt("onesrow_f", [1, 128], F32)
        epscol = SBt("epscol", [128, 2], F32)
        idn_bf = SBt("idn_bf", [128, 128], BF16)
        mask_bf = SBt("mask_bf", [128, 128], BF16)
        maska_bf = SBt("maska_bf", [128, 128], BF16)
        cst_f = SBt("cst_f", [128, 128], F32)
        sgc = SBt("sgc", [128, INTER // 128], F32)
        suc = SBt("suc", [128, INTER // 128], F32)
        sdc = SBt("sdc", [128, D // 128], F32)
        negc = SBt("negc", [128, max(HH, 2)], F32)
        nrowq = SBt("nrowq", [1, TGS], F32)
        nrowk = SBt("nrowk", [1, TGS], F32)
        t11a = SBt("t11a", [1, 2], F32)
        t11b = SBt("t11b", [1, 2], F32)
        c11bf = SBt("c11bf", [1, 2], BF16)
        srow = SBt("srow", [1, 512], F32)
        rrow = SBt("rrow", [1, 512], F32)
        rrowB = [SBt("rrowB0", [1, 512], F32), SBt("rrowB1", [1, 512], F32)]
        msq = SBt("msq", [1, 512], F32)
        murow = srow
        varow = rrow

        PSALL = PSB("PSALL", [128, 4096], F32)
        PS = [PSALL[:, i * 512:(i + 1) * 512] for i in range(6)]
        PSR = PSALL[:, 6 * 512:7 * 512]
        PST_ = PSALL[:, 7 * 512:8 * 512].bitcast(BF16)
        PSTH = [PST_[:, 0:512], PSR.bitcast(BF16)[:, 0:512]]

        class Carve:
            def __init__(self, ar):
                self.ar, self.off, self.limit = ar, 0, ar.shape[1]

            def take(self, elems, dt):
                sz = {F32: 4, BF16: 2, I8: 1}[dt]
                self.off = (self.off + 63) // 64 * 64
                v = self.ar[:, self.off:self.off + elems * sz].bitcast(dt)
                self.off += elems * sz
                assert self.off <= self.limit, (self.off, self.limit)
                return v

        ev_ps0 = P.op("ve", lambda e: e.memset(PSALL[:, :], 0.0), inc="ve")
        evc = P.dma("sy", cst_f[:, :], idn[:, :], "cs0")
        P.op("ve", lambda e: e.memset(ones_bf[:, :], 1.0), inc="ve")
        P.op("ve", lambda e: e.memset(negones_bf[:, :], -1.0), inc="ve")
        P.op("ve", lambda e: e.memset(onescol_f[:, :], 1.0), inc="ve")
        P.op("ve", lambda e: e.memset(onesrow_f[:, :], 1.0), inc="ve")
        P.op("ve", lambda e: e.memset(epscol[:, :], 1e-5), inc="ve")
        ev_idn = P.op("ve", lambda e: e.tensor_copy(idn_bf[:, :], cst_f[:, :]),
                      inc="ve", waits=[evc])
        evc2 = P.dma("sy", cst_f[:, :], maskT[:, :], "cs1", waits=[ev_idn])
        ev_mask = P.op("ve", lambda e: e.tensor_copy(mask_bf[:, :], cst_f[:, :]),
                       inc="ve", waits=[evc2])
        evc3 = P.dma("sy", cst_f[:, :], maskadd[:, :], "cs5", waits=[ev_mask])
        ev_maska = P.op("ve", lambda e: e.tensor_copy(maska_bf[:, :], cst_f[:, :]),
                        inc="ve", waits=[evc3])
        ev_sg = P.dma("sy", sgc[:, :], sg2d[:, :], "cs2")
        ev_su = P.dma("sy", suc[:, :], su2d[:, :], "cs3")
        ev_sd = P.dma("sy", sdc[:, :], sd2d[:, :], "cs4")

        cb = Carve(arb)
        wq_sb = cb.take(KC * Dh, BF16)
        wk_sb = cb.take(KC * Dh, BF16)
        wv_sb = cb.take(KC * Dh, BF16)
        cc_ = Carve(arc)
        wst2 = [cc_.take((KC // 4) * Dh, F32) for _ in range(2)]
        qkr = [cc_.take(512, BF16) for _ in range(3)]
        trev = [cc_.take(512, BF16) for _ in range(3)]
        vev = [cc_.take(512, BF16) for _ in range(2)]
        sqstrip = cc_.take(2 * HH * TT, F32)
        sqtrash = cc_.take(128, F32)
        nqk_sb = cc_.take(2 * HH, F32)
        mqh = cc_.take(128, F32)[0:8, :]
        mkh = cc_.take(128, F32)[0:8, :]
        mq1 = cc_.take(2, F32)[0:8, :]
        mk1 = cc_.take(2, F32)[0:8, :]
        ppc = Carve(arb)
        ppc.off = 65536
        ppi8 = [ppc.take(4096, I8) for _ in range(2)]
        ppbf = [ppc.take(4096, BF16) for _ in range(2)]
        cd = Carve(ard)
        xst = [cd.take(KC * 128, F32) for _ in range(2)]
        xbf = [cd.take(KC * 128, BF16) for _ in range(2)]
        ctile = [cd.take(128, F32) for _ in range(2)]
        stile = [cd.take(128, F32) for _ in range(2)]
        t1s = [cd.take(512, F32) for _ in range(2)]
        t2s = [cd.take(512, F32) for _ in range(2)]

        ev_wcast = None
        wst_free = [None, None]
        wi = 0
        for wdr, wsb in ((wqT, wq_sb), (wkT, wk_sb), (wvT, wv_sb)):
            for hf in range(4):
                wsl = wi % 2
                wi += 1
                cs = slice(hf * (KC // 4), (hf + 1) * (KC // 4))
                src = wdr.rearrange("(c p) j -> p c j", p=128)[:, cs, :]
                evl = P.dma("sy", wst2[wsl].rearrange("p (c j) -> p c j", j=Dh),
                            src, f"wl{wsl}", waits=[wst_free[wsl]])
                dst = wsb.rearrange("p (c j) -> p c j", j=Dh)[:, cs, :]
                ev_wcast = P.op(
                    "ve", lambda e, d=dst, s_=wst2[wsl]: e.tensor_copy(
                        d, s_.rearrange("p (c j) -> p c j", j=Dh)),
                    inc="ve", waits=[evl])
                wst_free[wsl] = ev_wcast
        ev_wstfree = ev_wcast

        pp_chunks = []
        woT3 = woT.rearrange("(c p) j -> p c j", p=128)
        for c_ in range(HH):
            for j0 in (0, 1024):
                src = woT3[:, c_, j0:j0 + 1024]
                dst = WOB[:, c_ * D + j0:c_ * D + j0 + 1024]
                pp_chunks.append(("f32", src, [(dst, None)], 1024, 1))
        NWO = len(pp_chunks)
        for wsrc, wdst, w_cols, cw in ((wg, WGs, INTER, PPW), (wu, WUs, INTER, PPW),
                                       (wd, WDs, D, PPD)):
            nch = wsrc.shape[0] // 128
            w4 = wdst.rearrange("p (b c u) -> p b c u", c=nch, u=128)
            for c0 in range(0, w_cols, cw):
                src = wsrc.rearrange("(c p) i -> p c i", p=128)[:, :, c0:c0 + cw]
                if cw >= 128:
                    nb = cw // 128
                    b0 = c0 // 128
                    dsts = [(wdst[:, (b0 + b) * nch * 128:
                                  (b0 + b + 1) * nch * 128],
                             (lambda buf, el, n=nch, bb=nb, b_=b:
                              buf[:, 0:el].rearrange(
                                  "p (c b u) -> p c b u", c=n, b=bb,
                                  u=128)[:, :, b_, :]))
                            for b in range(nb)]
                else:
                    u0 = c0 % 128
                    dsts = [(w4[:, c0 // 128, :, u0:u0 + cw],
                             (lambda buf, el, n=nch: buf[:, 0:el].rearrange(
                                 "p (c i) -> p c i", c=n)))]
                pp_chunks.append(("i8", src, dsts, nch * cw, nch))
        pp_state = dict(i=0, ev_cast=[None, None], ev_spill=[None, None],
                        bar0=None)

        def pump_prepass(nsteps):
            for _ in range(nsteps):
                i = pp_state["i"]
                if i >= len(pp_chunks):
                    return
                kind, src, dsts, elems, nch = pp_chunks[i]
                sl = i % 2
                if kind == "f32":
                    ldbuf = ppi8[sl].bitcast(F32)[:, 0:elems]
                    cbuf = ppbf[sl][:, 0:elems]
                    evl = P.dma("sy", ldbuf, src, f"pp{sl}",
                                waits=[pp_state["ev_cast"][sl]] +
                                      (pp_state.get("bar0") or []
                                       if i < 2 else []))
                    evcst = P.op("gp", lambda e, a=cbuf, b=ldbuf:
                                 e.tensor_copy(a, b), inc="gp",
                                 waits=[evl, pp_state["ev_spill"][sl]])
                    pp_state["ev_cast"][sl] = evcst
                    evs = P.dma("gp", dsts[0][0], cbuf, f"pq{sl}",
                                waits=[evcst])
                else:
                    evl = P.dma("sy", ppi8[sl][:, 0:elems].rearrange(
                        "p (c i) -> p c i", c=nch), src, f"pp{sl}",
                        waits=[pp_state["ev_cast"][sl]] +
                              (pp_state.get("bar0") or [] if i < 2 else []))
                    evcst = P.op("gp", lambda e, a=ppbf[sl][:, 0:elems],
                                 b=ppi8[sl][:, 0:elems]: e.tensor_copy(a, b),
                                 inc="gp", waits=[evl, pp_state["ev_spill"][sl]])
                    pp_state["ev_cast"][sl] = evcst
                    evs = None
                    for dst, view in dsts:
                        evs = P.dma("gp", dst, view(ppbf[sl], elems), f"pq{sl}",
                                    waits=[evcst])
                pp_state["ev_spill"][sl] = evs
                pp_state["i"] = i + 1
                if i == NWO - 1:
                    evA = P.dma("sy", wo_sbA, WOB[:, 0:(HH // 2) * D], "wld",
                                waits=[pp_state["ev_spill"][0],
                                       pp_state["ev_spill"][1]])
                    P.dma("sy", wo_sbB, WOB[:, (HH // 2) * D:], "wld")

        bank_rel = {i: None for i in range(6)}
        pst_rel = [None, None]
        pend_tr = []
        slot_rel = dict(t1=[None] * 2, t2=[None] * 2, qkr=[None] * 3,
                        xst=[None, None], xbf=[None, None])
        sq_slot = [None] * 3
        trv_sem = [None] * 3
        vev_sem = [None, None]
        rope_ct = 0
        tr_ct = [0]
        vev_ct = 0

        xT3 = xT.rearrange("(c p) t -> p c t", p=128)
        cos3 = cos_tm.rearrange("(tt p) j -> p tt j", p=128)
        sin3 = sin_sgn.rearrange("(tt p) j -> p tt j", p=128)

        for t in range(TT):
            sl = t % 2
            evx = P.dma("sy", xst[sl].rearrange("p (c u) -> p c u", u=128),
                        xT3[:, :, t * 128:(t + 1) * 128], f"xs{sl}",
                        waits=[slot_rel["xst"][sl]])
            evcos = P.dma("sy", ctile[sl][:, :], cos3[:, t, :], f"xs{sl}")
            evsin = P.dma("sy", stile[sl][:, :], sin3[:, t, :], f"xs{sl}")
            evcast = P.op("ve", lambda e, a=xbf[sl], b=xst[sl]: e.tensor_copy(a, b),
                          inc="ve", waits=[evsin, slot_rel["xbf"][sl]])

            for jb in range(NJB):
                bs = ((t * NJB + jb) % 2) * 3
                qb, kb, vb = PS[bs], PS[bs + 1], PS[bs + 2]
                P.waits("te", [ev_ps0, evcast, ev_wcast, bank_rel[bs],
                               bank_rel[bs + 1], bank_rel[bs + 2]])
                evq = evk = evv = None
                for kc in range(KC):
                    lhs = xbf[sl].rearrange("p (c u) -> p c u", u=128)[:, kc, :]
                    st_ = (kc == 0)
                    sp_ = (kc == KC - 1)
                    for wi, (bnk, wsb) in enumerate(((qb, wq_sb), (kb, wk_sb),
                                                     (vb, wv_sb))):
                        rhs = wsb.rearrange("p (c j) -> p c j", j=Dh)[
                            :, kc, jb * JB:(jb + 1) * JB]
                        ev = P.op("te", lambda e, o=bnk[:, 0:JB], l=lhs, r=rhs,
                                  a=st_, b=sp_: e.matmul(o, l, r, start=a, stop=b),
                                  inc="pe" if sp_ else None)
                        if sp_:
                            if wi == 0:
                                evq = ev
                            elif wi == 1:
                                evk = ev
                            else:
                                evv = ev

                for pi, (bnk, evp, dstD) in enumerate(((qb, evq, Qs), (kb, evk, Ks))):
                    s2 = rope_ct % 3
                    rope_ct += 1
                    cosb = ctile[sl].rearrange("p j -> p () j").broadcast_to(
                        (128, JBH, 128))
                    sinb = stile[sl].rearrange("p j -> p () j").broadcast_to(
                        (128, JBH, 128))
                    if "norope" in ablate:
                        ev1 = P.op("ve", lambda e, o=t1s[s2][:, 0:JB],
                                   i=bnk[:, 0:JB]: e.tensor_copy(o, i),
                                   inc="ve", waits=[evp, slot_rel["t1"][s2]])
                        ev2 = P.op("ve", lambda e, o=t2s[s2][:, 0:JB],
                                   i=bnk[:, 0:JB]: e.tensor_copy(o, i),
                                   inc="ve", waits=[slot_rel["t2"][s2]])
                    else:
                        s3_ = rope_ct % 2
                        ev1 = P.op("ve", lambda e, o=t1s[s3_][:, 0:JB], i=bnk[:, 0:JB],
                                   cb_=cosb: e.scalar_tensor_tensor(
                                       o, i, 1.0, cb_, ALU.bypass, ALU.mult),
                                   inc="ve", waits=[evp, slot_rel["t1"][s3_]])
                        swap = bnk[:, 0:JB].rearrange("p (a b) -> p a b", b=2)[:, :, ::-1]
                        ev2 = P.op("ve", lambda e, o=t2s[s3_][:, 0:JB], i=swap,
                                   sb_=sinb: e.scalar_tensor_tensor(
                                       o, i, 1.0, sb_, ALU.bypass, ALU.mult),
                                   inc="ve", waits=[slot_rel["t2"][s3_]])
                    bank_rel[bs + pi] = ev2
                    eva = P.op("ve", lambda e, o=qkr[s2][:, 0:JB],
                               a=t1s[s3_][:, 0:JB], b=t2s[s3_][:, 0:JB]:
                               e.tensor_add(o, a, b),
                               inc="ve", waits=[ev2, slot_rel["qkr"][s2],
                                                sq_slot[s2]])
                    slot_rel["t1"][s3_] = eva
                    slot_rel["t2"][s3_] = eva
                    evsq_ = None
                    for c4 in range(JBH):
                        col = pi * (HH * TT) + (jb * JBH + c4) * TT + t
                        evsq_ = P.op("sc", lambda e, i=qkr[s2][:, c4 * 128:
                                                               (c4 + 1) * 128],
                                     a=sqstrip[:, col:col + 1]:
                                     e.activation(sqtrash, i, AF.Square,
                                                  accum_out=a),
                                     inc="sc", waits=[eva])
                    sq_slot[s2] = evsq_
                    if "notr" in ablate:
                        slot_rel["qkr"][s2] = eva
                        continue

                    def _mk_tr(s2=s2, eva=eva, t=t, jb=jb, dstD=dstD):
                        ph = tr_ct[0] % 2
                        tr_ct[0] += 1
                        P.waits("te", [eva, ev_idn, pst_rel[ph]])
                        evtr = None
                        for sub in range(JBH):
                            evtr = P.op("te", lambda e,
                                        o=PSTH[ph][:, sub * 128:(sub + 1) * 128],
                                        i=qkr[s2][:, sub * 128:(sub + 1) * 128]:
                                        e.transpose(o, i, idn_bf[:, :]),
                                        inc="pe" if sub == JBH - 1 else None)
                        slot_rel["qkr"][s2] = evtr
                        evcp = P.op("sc", lambda e, o=trev[s2][:, 0:JB],
                                    i=PSTH[ph][:, 0:JB]: e.copy(o, i),
                                    inc="sc", waits=[evtr, trv_sem[s2]])
                        pst_rel[ph] = evcp
                        dst = dstD[jb * JB:(jb + 1) * JB,
                                   t * 128:(t + 1) * 128].rearrange(
                                       "(s p) t -> p s t", p=128)
                        evd = P.dma("sc", dst, trev[s2][:, 0:JB].rearrange(
                            "p (s u) -> p s u", u=128), f"tr{s2}", waits=[evcp])
                        trv_sem[s2] = evd
                    pend_tr.append(_mk_tr)
                    if len(pend_tr) > 2:
                        pend_tr.pop(0)()
                s5 = vev_ct % 2
                vev_ct += 1
                evvc = P.op("sc", lambda e, o=vev[s5][:, 0:JB], i=vb[:, 0:JB]:
                            e.copy(o, i), inc="sc", waits=[evv, vev_sem[s5]])
                bank_rel[bs + 2] = evvc
                vdst = Vs.rearrange("p (h kt j) -> p h kt j", h=HH, j=128)[
                    :, jb * JBH:(jb + 1) * JBH, t, :]
                evvd = P.dma("sc", vdst, vev[s5][:, 0:JB].rearrange(
                    "p (h j) -> p h j", j=128), f"vv{s5}", waits=[evvc])
                vev_sem[s5] = evvd
            slot_rel["xbf"][sl] = ("pe", P.cnt["pe"])
            slot_rel["xst"][sl] = ("ve", P.cnt["ve"])
        for fn_ in pend_tr:
            fn_()
        pend_tr.clear()

        phaseA_loads_done = [P.final("tr0"), P.final("tr1"), P.final("tr2"),
                             P.final("vv0"), P.final("vv1")]
        phaseA_loads_done = [e for e in phaseA_loads_done if e[1] > 0]
        barA = P.bar()
        pp_state["bar0"] = barA

        evmx = []
        for pi2 in range(2):
            for h2 in range(HH):
                c0_ = (pi2 * HH + h2) * TT
                evmx.append(P.op("ve", lambda e,
                                 o=nqk_sb[:, pi2 * HH + h2:pi2 * HH + h2 + 1],
                                 i=sqstrip[:, c0_:c0_ + TT]:
                                 e.reduce_max(o, i, mybir.AxisListType.X),
                                 inc="ve", waits=barA))
        evnd = P.dma("sy", NQKD[:, :], nqk_sb[:, :], "nq", waits=evmx)
        NQT = NQKD.rearrange("p c -> c p")
        evlq = P.op("sy", lambda e: e.dma_start(
            out=mqh[0:8, :], in_=NQT[0:8, :],
            allow_slow_non_contiguous=True), inc="nq", n=16, waits=[evnd])
        evlk = P.op("sy", lambda e: e.dma_start(
            out=mkh[0:8, :], in_=NQT[8:16, :],
            allow_slow_non_contiguous=True), inc="nq", n=16)
        evr1 = P.op("ve", lambda e: e.reduce_max(mq1[0:8, 0:1], mqh[0:8, :],
                                                 mybir.AxisListType.X),
                    inc="ve", waits=[evlk])
        evr2_ = P.op("ve", lambda e: e.reduce_max(mk1[0:8, 0:1], mkh[0:8, :],
                                                  mybir.AxisListType.X),
                    inc="ve", waits=[evlk])
        evr3 = P.op("ve", lambda e: e.tensor_mul(mq1[0:8, 0:1], mq1[0:8, 0:1],
                                                 mk1[0:8, 0:1]),
                    inc="ve", waits=[evr1, evr2_])
        evr4 = P.op("sc", lambda e: e.activation(mk1[0:8, 0:1], mq1[0:8, 0:1],
                                                 AF.Ln, scale=1.0 / 128.0),
                    inc="sc", waits=[evr3])
        evr5 = P.op("sc", lambda e: e.activation(mq1[0:8, 0:1], mk1[0:8, 0:1],
                                                 AF.Exp, scale=0.5),
                    inc="sc", waits=[evr4])
        evr6 = P.op("ve", lambda e: e.tensor_scalar(mq1[0:8, 0:1], mq1[0:8, 0:1],
                                                    -1.0, None, ALU.mult),
                    inc="ve", waits=[evr5])
        evst_ = P.dma("sy", CND.rearrange("x h -> h x"), mq1[0:8, 0:1], "nq",
                      waits=[evr6])
        negc_ready = P.dma("sy", negc[:, 0:HH], CND.broadcast_to((128, HH)),
                           "nq", waits=[evst_])
        if stop_after == 'A':
            P.waits("sy", phaseA_loads_done + barA +
                    [P.final("pq0"), P.final("pq1")])
            P.dma("sy", outT[0:128, 0:128], cst_f[:, :].bitcast(F32) if False else cst_f[:, :], "out")
            P.wait("sy", P.final("out"))
            return _emit(nc, P, es)

        cb2 = Carve(arb)
        o_sb = cb2.take(HH * S, BF16)
        cc2 = Carve(arc)
        qh = [cc2.take(S, BF16) for _ in range(2)]
        kh = [cc2.take(S, BF16) for _ in range(2)]
        vh = [cc2.take(S, BF16) for _ in range(2)]
        sqs = cc2.take(S, BF16)
        cd2 = Carve(ard)
        ptp = [cd2.take(1024, BF16) for _ in range(3)]
        bc_sb = [cd2.take(512, F32) for _ in range(2)]
        cdwo = Carve(ard)
        cdwo.off = 10240
        wo_sbA = cdwo.take((HH // 2) * D, BF16)
        ccwo = Carve(arc)
        ccwo.off = 49152
        wo_sbB = ccwo.take((HH // 2) * D, BF16)

        qkv_rel = [None, None]
        P.waits("sy", phaseA_loads_done + barA)

        SPAIR = [PSALL[:, 0:1024], PSALL[:, 1024:2048]]
        OB = [PS[4], PS[5]]
        RBF = [PSR[:, 0:512], PST_.bitcast(F32)[:, 0:512]]
        RB = [RBF[0][0:1, :], RBF[1][0:1, :]]
        o_evac_ev = None
        ptp_rel = [None, None, None]
        psb_exp = [None, None]
        ob_rel = [None, None]
        rb_rel = [None, None]
        rrowd_rel = [None, None]
        rrow_war = [None, None]
        bc_rel = [None, None]
        pend_epi = []
        gctl = [0]

        Vs5 = Vs.rearrange("p (h kt j) -> p h kt j", h=HH, j=128)

        load_evs = {}

        def emit_loads(h):
            sl = h % 2
            evq = P.dma("sy", qh[sl], Qs[h * 128:(h + 1) * 128, :], f"qh{sl}",
                        waits=[qkv_rel[sl]] + ([negc_ready] if h == 1 else []))
            evk = P.dma("sy", kh[sl], Ks[h * 128:(h + 1) * 128, :], f"qh{sl}")
            evv = P.dma("sy", vh[sl].rearrange("p (kt j) -> p kt j", j=128),
                        Vs5[:, h, :, :], f"qh{sl}")
            load_evs[h] = evv

        emit_loads(0)
        for h in range(HH):
            sl = h % 2
            evv = load_evs[h]
            negc_ev = negc_ready

            for g in range(NG):
                t0 = 4 * g
                kmax = t0 + 3
                P2 = (kmax + 1) // 2
                ob = gctl[0] % 2
                gctl[0] += 1

                def _emit_pair_S(p, first):
                    slot = p % 2
                    ev = None
                    for half in (0, 1):
                        K = 2 * p + half
                        qlo = max(0, K - t0) * 128
                        diag = K >= t0
                        w0 = [psb_exp[slot]]
                        if first and half == 0:
                            w0 += [ob_rel[ob], rb_rel[ob], ev_maska]
                            if t0 == 0:
                                w0 += [evv]
                        ev = P.op("te", lambda e,
                                  o=SPAIR[slot][:, half * 512 + qlo:
                                                half * 512 + 512],
                                  l=kh[sl][:, K * 128:(K + 1) * 128],
                                  r=qh[sl][:, g * 512 + qlo:(g + 1) * 512],
                                  d=diag:
                                  e.matmul(o, l, r, start=True, stop=not d),
                                  inc=None if diag else
                                      ("pe" if half == 1 else None), waits=w0)
                        if diag:
                            ev = P.op("te", lambda e,
                                      o=SPAIR[slot][:, half * 512 + qlo:
                                                    half * 512 + qlo + 128]:
                                      e.matmul(o, maska_bf[:, :], idn_bf[:, :],
                                               start=False, stop=True),
                                      inc="pe" if half == 1 else None)
                    return ev

                def _consume(p):
                    slot = p % 3
                    evPV = None
                    for half in (0, 1):
                        K = 2 * p + half
                        qlo = max(0, K - t0) * 128
                        pslice = ptp[slot][:, half * 512 + qlo:half * 512 + 512]
                        P.waits("te", [exp_evs[p]])
                        P.op("te", lambda e, r=pslice, a=(K == 0),
                             b=(K == kmax), ql=qlo, rb_=RB[ob]:
                             e.matmul(rb_[0:1, ql:512], ones_bf[:, 0:1], r,
                                      start=a, stop=b))
                        evPV = P.op("te", lambda e, o=OB[ob][:, qlo:512],
                                    l=vh[sl][:, K * 128:(K + 1) * 128],
                                    r=pslice, a=(K == 0), b=(K == kmax):
                                    e.matmul(o, l, r, start=a, stop=b),
                                    inc="pe")
                    ptp_rel[slot] = evPV
                    return evPV

                exp_evs = {}
                evS_next = _emit_pair_S(0, True)
                evpv_stop = None
                for p in range(P2):
                    s2_ = p % 2
                    s3_ = p % 3
                    evS = evS_next
                    if p < P2 - 1:
                        evS_next = _emit_pair_S(p + 1, False)
                    if p >= 1:
                        _consume(p - 1)
                    qlo0 = max(0, 2 * p - t0) * 128
                    evE = P.op("sc", lambda e, o=ptp[s3_][:, qlo0:1024],
                               i=SPAIR[s2_][:, qlo0:1024], b=negc[:, h:h + 1]:
                               e.activation(o, i, AF.Exp, bias=b, scale=RSQRT_DK),
                               inc="sc", waits=[evS, negc_ev, ptp_rel[s3_]])
                    exp_evs[p] = evE
                    psb_exp[s2_] = evE
                evpv_stop = _consume(P2 - 1)

                evl2 = P.op("sc", lambda e, ob_=ob: e.activation(
                    rrowB[ob_][0:1, :], RB[ob_][0:1, 0:512], AF.Ln),
                            inc="sc", waits=[evpv_stop, rrow_war[ob]])
                rb_rel[ob] = evl2
                evr2 = P.op("sc", lambda e, ob_=ob: e.activation(
                    rrowB[ob_][0:1, :], rrowB[ob_][0:1, :], AF.Exp, scale=-1.0),
                            inc="sc", waits=[evl2])
                evd1 = P.dma("sc", RROWD[ob:ob + 1, :], rrowB[ob][0:1, :],
                             f"rb{ob}", waits=[evr2, rrowd_rel[ob]])
                rrow_war[ob] = evd1
                evd2 = P.dma("sc", bc_sb[ob][:, :],
                             RROWD[ob:ob + 1, :].broadcast_to((128, 512)),
                             f"rc{ob}", waits=[evd1, bc_rel[ob]])
                rrowd_rel[ob] = evd2
                oe = P.op("ve", lambda e, o=o_sb[:, h * S + g * 512:
                                                 h * S + (g + 1) * 512],
                          ob_=ob: e.scalar_tensor_tensor(
                              o, OB[ob_][:, 0:512], 1.0, bc_sb[ob_][:, :],
                              ALU.bypass, ALU.mult),
                          inc="ve", waits=[evpv_stop, evd2])
                ob_rel[ob] = oe
                bc_rel[ob] = oe
                o_evac_ev = oe
                if g == 0 and h + 1 < HH:
                    emit_loads(h + 1)
            qkv_rel[sl] = evpv_stop
            if "nopp" not in ablate:
                pump_prepass(12 if HH >= 8 else 96)
        o_evac_ev = P.final("ve")
        psr_rel = rb_rel[0]
        ps2_rel = ("sc", P.cnt["sc"])
        if "nopp" not in ablate:
            pump_prepass(len(pp_chunks))

        barB = P.bar()
        if stop_after == 'B':
            P.waits("sy", barB)
            P.dma("sy", outT[0:128, 0:128], cst_f[:, :], "out")
            P.wait("sy", P.final("out"))
            return _emit(nc, P, es)

        cd3 = Carve(ard)
        yst = [cd3.take(512, BF16) for _ in range(2)]

        ev_wo = P.final("wld")

        ybank_rel = [None, None]
        yst_sem = [None, None]
        yct = 0
        evcc = [None, None, None, None]
        groups = [[i, i + 1] for i in range(0, NC, 2)]
        TQ = TT // 4
        for tt in range(TT):
            for j4 in range(ND4):
                yb = yct % 2
                ps = PS[yb]
                P.waits("te", [ev_wo, o_evac_ev, ybank_rel[yb]])
                evy = None
                for dc in range(HH):
                    lhs = o_sb[:, dc * S + tt * 128:dc * S + (tt + 1) * 128]
                    wsb = wo_sbA if dc < HH // 2 else wo_sbB
                    rhs = wsb.rearrange("p (c j) -> p c j", j=D)[
                        :, dc % (HH // 2), j4 * 512:(j4 + 1) * 512]
                    evy = P.op("te", lambda e, o=ps[:, 0:512], l=lhs, r=rhs,
                               a=(dc == 0), b=(dc == HH - 1):
                               e.matmul(o, l, r, start=a, stop=b),
                               inc="pe" if dc == HH - 1 else None)
                evcp = P.op("sc", lambda e, o=yst[yb][:, :], i=ps[:, 0:512]:
                            e.copy(o, i), inc="sc", waits=[evy, yst_sem[yb]])
                ybank_rel[yb] = evcp
                evd = P.dma("sc", YP[tt * 128:(tt + 1) * 128,
                                     j4 * 512:(j4 + 1) * 512],
                            yst[yb][:, :], f"yp{yb}", waits=[evcp])
                yst_sem[yb] = evd
                yct += 1
            if (tt + 1) % TQ == 0:
                q = tt // TQ
                ypq = YP[q * (S // 4):(q + 1) * (S // 4), :]
                ysq = YSH[q * (S // 8):(q + 1) * (S // 8), :]
                evcc[q] = P.op("gp", lambda e, a=ypq, b=ysq:
                               e.collective_compute("ReduceScatter", ALU.add,
                                                    groups, [a.opt()], [b.opt()]),
                               inc=f"cc{q}", n=1,
                               waits=[P.final("yp0"), P.final("yp1")])

        if stop_after == 'W':
            P.waits("sy", [P.final("yp0"), P.final("yp1")])
            P.dma("sy", outT[0:128, 0:128], cst_f[:, :], "out")
            P.wait("sy", P.final("out"))
            return _emit(nc, P, es)
        if stop_after == 'R':
            for q in range(4):
                P.wait("gp", P.final(f"cc{q}"))
            P.dma("gp", outT[0:128, 0:128], cst_f[:, :], "out")
            P.wait("gp", P.final("out"))
            return _emit(nc, P, es)
        barC1 = P.bar()
        wo_end = ("pe", P.cnt["pe"])
        cb3 = Carve(arb)
        cb3.off = 81920
        h1t = [cb3.take(D, F32) for _ in range(2)]
        ccX = Carve(arc)
        x2t = [ccX.take(D, BF16) for _ in range(2)]
        cdT = Carve(ard)
        cdT.off = 33280
        mucol = cdT.take(2, F32)
        sqcol = cdT.take(2, F32)
        varcol = cdT.take(2, F32)
        rstdcol = cdT.take(2, F32)
        msqcol = cdT.take(2, F32)
        ccL = Carve(arc)
        ccL.off = 8192
        ysr = [ccL.take(D, BF16) for _ in range(2)]
        xhr = [ccL.take(D, F32) for _ in range(2)]
        cd4 = Carve(ard)
        cd4.off = 2 * 1024 + 128
        trev2 = [cd4.take(512, BF16) for _ in range(2)]

        ln_rel = [None, None]
        x2t_rel = [None, None]
        trv2_sem = [None, None]
        pst2_rel = [None, None]
        sq1_rel = None
        THQ = TH // 4
        for tt in range(TH):
            sl = tt % 2
            evy = P.dma("sy", ysr[sl], YSH[tt * 128:(tt + 1) * 128, :], f"ln{sl}",
                        waits=[evcc[tt // THQ], ln_rel[sl], ev_wo,
                               P.final("pq0"), P.final("pq1")])
            evxh = P.dma("sy", xhr[sl], xh[tt * 128:(tt + 1) * 128, :], f"ln{sl}")
            evh1 = P.op("ve", lambda e, o=h1t[sl], a=ysr[sl], b=xhr[sl]:
                        e.tensor_add(o, a, b), inc="ve",
                        waits=[evxh, P.final("pq0"), P.final("pq1")])
            evmu = P.op("ve", lambda e, o=mucol[:, 0:1], i=h1t[sl]:
                        e.tensor_reduce(o, i, mybir.AxisListType.X, ALU.add),
                        inc="ve", waits=[evh1])
            evsq = P.op("sc", lambda e, i=h1t[sl], o=ysr[sl]: e.activation(
                o, i, AF.Square, accum_out=sqcol[:, 0:1]),
                inc="sc", waits=[evh1, sq1_rel])
            evmu2 = P.op("ve", lambda e: e.tensor_scalar(
                mucol[:, 0:1], mucol[:, 0:1], 1.0 / D, None, ALU.mult),
                inc="ve", waits=[evmu])
            evms = P.op("sc", lambda e: e.activation(msqcol[:, 0:1], mucol[:, 0:1],
                                                     AF.Square),
                        inc="sc", waits=[evmu2])
            evvar = P.op("ve", lambda e: e.scalar_tensor_tensor(
                varcol[:, 0:1], sqcol[:, 0:1], 1.0 / D, msqcol[:, 0:1],
                ALU.mult, ALU.subtract), inc="ve", waits=[evsq, evms])
            evlnv = P.op("sc", lambda e: e.activation(rstdcol[:, 0:1],
                                                      varcol[:, 0:1], AF.Ln,
                                                      bias=epscol[:, 0:1]),
                         inc="sc", waits=[evvar])
            evrst = P.op("sc", lambda e: e.activation(rstdcol[:, 0:1],
                                                      rstdcol[:, 0:1], AF.Exp,
                                                      scale=-0.5),
                         inc="sc", waits=[evlnv])
            evx2 = P.op("ve", lambda e, o=x2t[sl], i=h1t[sl]: e.tensor_scalar(
                o, i, mucol[:, 0:1], rstdcol[:, 0:1], ALU.subtract, ALU.mult),
                inc="ve", waits=[evrst, x2t_rel[sl], wo_end])
            sq1_rel = evx2
            ln_rel[sl] = evx2
            for dg in range(ND4):
                ph = (tt * ND4 + dg) % 2
                P.waits("te", [evx2, pst2_rel[ph]])
                evtr = None
                for sub in range(4):
                    evtr = P.op("te", lambda e,
                                o=PSTH[ph][:, sub * 128:(sub + 1) * 128],
                                i=x2t[sl][:, dg * 512 + sub * 128:
                                          dg * 512 + (sub + 1) * 128]:
                                e.transpose(o, i, idn_bf[:, :]),
                                inc="pe" if sub == 3 else None)
                if dg == ND4 - 1:
                    x2t_rel[sl] = evtr
                evcp = P.op("sc", lambda e, o=trev2[ph][:, 0:512],
                            i=PSTH[ph][:, 0:512]: e.copy(o, i),
                            inc="sc", waits=[evtr, trv2_sem[ph]])
                pst2_rel[ph] = evcp
                dst = X2s.rearrange("p (q c u) -> p q c u", c=KC, u=512)[
                    :, tt // THQ, dg * 4:(dg + 1) * 4,
                    (tt % THQ) * 128:(tt % THQ) * 128 + 128]
                evd = P.dma("sc", dst, trev2[ph][:, 0:512].rearrange(
                    "p (s u) -> p s u", u=128), f"x2q{tt // THQ}_{ph}",
                    waits=[evcp])
                trv2_sem[ph] = evd

        x2_q_done = [[P.final(f"x2q{q}_0"), P.final(f"x2q{q}_1")]
                     for q in range(4)]
        barC2 = P.bar()
        if stop_after == 'C':
            P.waits("sy", [e for q in x2_q_done for e in q] + barC2)
            P.dma("sy", outT[0:128, 0:128], cst_f[:, :], "out")
            P.wait("sy", P.final("out"))
            return _emit(nc, P, es)

        cb4 = Carve(arb)
        fuse = cb4.take(ICB * 512, BF16)
        x2fA = cb4.take(KC * 512, BF16)
        cb4b = Carve(arb)
        cb4b.off = 81920
        x2fB = cb4b.take(KC * 512, BF16)
        x2fr = [x2fA, x2fB]
        cc4 = Carve(arc)
        zar = cc4.take(KC * 512, F32)
        wdsl = [cc4.take(ICB * 128, BF16) for _ in range(2)]
        cd5 = Carve(ard)
        cd5.off = 2 * 2048 + 128 + 2 * 1024 + 64
        wgsl = [cd5.take(KC * 128, BF16) for _ in range(2)]
        wusl = [cd5.take(KC * 128, BF16) for _ in range(2)]
        silu_sb = [cd5.take(512, BF16) for _ in range(2)]
        sqf = cd5.take(512, F32)
        bmu = cd5.take(512, F32)
        brs = cd5.take(512, F32)
        tmpn = cd5.take(512, F32)

        PSTF = PSALL[:, 7 * 512:8 * 512]
        WG4 = WGs.rearrange("p (b c u) -> p b c u", c=KC, u=128)
        WU4 = WUs.rearrange("p (b c u) -> p b c u", c=KC, u=128)
        WD4 = WDs.rearrange("p (b c u) -> p b c u", c=ICB, u=128)
        X2T4 = X2s.rearrange("p (q c u) -> p q c u", c=KC, u=512)
        out3 = outT.rearrange("(c p) q -> p c q", p=128)

        P.waits("sy", [P.final("pq0"), P.final("pq1")])
        x2f_sem = None
        fuse_rel = None
        x2f_war = [None, None]
        gsl_rel = [None, None]
        usl_rel = [None, None]
        silu_rel = [None, None]
        wdsl_rel = [None, None]
        gb_rel = [None, None]
        ub_rel = [None, None]
        zb_rel = [None, None]
        st_rel = None
        bcn_rel = None
        lnb_war = None
        out_sem = None
        for qg in range(QG):
            x2f = x2fr[qg % 2]
            qw = THQ * 128
            q_lo = (qg * 512) // qw
            q_hi = ((qg + 1) * 512 - 1) // qw
            need_q = [e for q_ in range(q_lo, q_hi + 1) for e in x2_q_done[q_]]
            evxf = P.dma("sy", x2f.rearrange("p (c u) -> p c u", u=512),
                         X2T4[:, qg, :, :], "x2l",
                         waits=need_q + ([wo_end, P.final("pq0"),
                                          P.final("pq1")] if qg <= 1 else []) +
                               (x2_q_done[3] if qg == 1 else []) +
                               (x2f_war[qg % 2] or []))
            for icb in range(ICB):
                sl = icb % 2
                evg = P.dma("sy", wgsl[sl].rearrange("p (c u) -> p c u", u=128),
                            WG4[:, icb, :, :], f"fg{sl}",
                            waits=[gsl_rel[sl]] +
                                  ([wo_end] if qg == 0 and icb < 2 else []))
                evu = P.dma("sy", wusl[sl].rearrange("p (c u) -> p c u", u=128),
                            WU4[:, icb, :, :], f"fu{sl}",
                            waits=[usl_rel[sl]])
                gb = PS[sl]
                ub = PS[2 + sl]
                P.waits("te", [evg, evu, evxf, gb_rel[sl], ub_rel[sl]])
                evgm = evum = None
                for kc in range(KC):
                    rhs = x2f.rearrange("p (c u) -> p c u", u=512)[:, kc, :]
                    evgm = P.op("te", lambda e, o=gb[:, 0:512],
                                l=wgsl[sl].rearrange("p (c u) -> p c u",
                                                     u=128)[:, kc, :], r=rhs,
                                a=(kc == 0), b=(kc == KC - 1):
                                e.matmul(o, l, r, start=a, stop=b),
                                inc="pe" if kc == KC - 1 else None)
                for kc in range(KC):
                    rhs = x2f.rearrange("p (c u) -> p c u", u=512)[:, kc, :]
                    evum = P.op("te", lambda e, o=ub[:, 0:512],
                                l=wusl[sl].rearrange("p (c u) -> p c u",
                                                     u=128)[:, kc, :], r=rhs,
                                a=(kc == 0), b=(kc == KC - 1):
                                e.matmul(o, l, r, start=a, stop=b),
                                inc="pe" if kc == KC - 1 else None)
                gsl_rel[sl] = evgm
                usl_rel[sl] = evum
                evsi = P.op("sc", lambda e, o=silu_sb[sl], i=gb[:, 0:512],
                            s_=sgc[:, icb:icb + 1]: e.activation(
                                o, i, AF.Silu, scale=s_),
                            inc="sc", waits=[evgm, ev_sg, silu_rel[sl]])
                gb_rel[sl] = evsi
                evfu = P.op("ve", lambda e, o=fuse[:, icb * 512:(icb + 1) * 512],
                            i=ub[:, 0:512], s_=suc[:, icb:icb + 1],
                            si=silu_sb[sl]: e.scalar_tensor_tensor(
                                o, i, s_, si, ALU.mult, ALU.mult),
                            inc="ve", waits=[evum, ev_su, evsi, fuse_rel])
                ub_rel[sl] = evfu
                silu_rel[sl] = evfu
            fuse_done = P.final("ve")

            for dcb in range(KC):
                sl = dcb % 2
                evw = P.dma("sy", wdsl[sl].rearrange("p (c u) -> p c u", u=128),
                            WD4[:, dcb, :, :], f"fd{sl}",
                            waits=[wdsl_rel[sl]] +
                                  ([wo_end] if qg == 0 and dcb < 2 else []))
                zb = PS[4 + sl]
                P.waits("te", [evw, ("ve", fuse_done[1]), zb_rel[sl]] +
                        ([o_evac_ev] if qg == 0 and dcb < 2 else []))
                evz = None
                for ic in range(ICB):
                    evz = P.op("te", lambda e, o=zb[:, 0:512],
                               l=wdsl[sl].rearrange("p (c u) -> p c u",
                                                    u=128)[:, ic, :],
                               r=fuse[:, ic * 512:(ic + 1) * 512],
                               a=(ic == 0), b=(ic == ICB - 1):
                               e.matmul(o, l, r, start=a, stop=b),
                               inc="pe" if ic == ICB - 1 else None)
                wdsl_rel[sl] = evz
                evzs = P.op("ve", lambda e, o=zar.rearrange(
                    "p (c u) -> p c u", u=512)[:, dcb, :], i=zb[:, 0:512],
                    s_=sdc[:, dcb:dcb + 1],
                    x_=x2f.rearrange("p (c u) -> p c u", u=512)[:, dcb, :]:
                    e.scalar_tensor_tensor(o, i, s_, x_, ALU.mult, ALU.add),
                    inc="ve", waits=[evz, ev_sd, out_sem] +
                                    (x2_q_done[3] if qg == 0 and
                                     dcb == 0 else []))
                zb_rel[sl] = evzs
                evq2 = P.op("sc", lambda e, i=zar.rearrange(
                    "p (c u) -> p c u", u=512)[:, dcb, :]: e.activation(
                        sqf, i, AF.Square), inc="sc", waits=[evzs, st_rel])
                evm1 = P.op("te", lambda e, i=zar.rearrange(
                    "p (c u) -> p c u", u=512)[:, dcb, :], a=(dcb == 0),
                    b=(dcb == KC - 1): e.matmul(
                        PSR[0:1, 0:512], onescol_f[:, 0:1], i, start=a, stop=b),
                    inc=None, waits=[evzs, (psr_rel if dcb == 0 else None)] +
                                    (x2_q_done[3] if qg == 0 and
                                     dcb == 0 else []))
                evm2 = P.op("te", lambda e, a=(dcb == 0), b=(dcb == KC - 1):
                            e.matmul(PSTF[0:1, 0:512], onescol_f[:, 0:1], sqf,
                                     start=a, stop=b),
                            inc="pe", waits=[evq2,
                                             (ps2_rel if dcb == 0 else None)] +
                                            (x2_q_done[3] if qg == 0 and
                                             dcb == 0 else []))
                st_rel = evm2
            evmr = P.op("ve", lambda e: e.tensor_scalar(
                murow[0:1, :], PSR[0:1, 0:512], 1.0 / D, None, ALU.mult),
                inc="ve", waits=[st_rel])
            psr_rel = evmr
            evms2 = P.op("sc", lambda e: e.activation(msq[0:1, :], murow[0:1, :],
                                                      AF.Square),
                         inc="sc", waits=[evmr])
            evvr = P.op("ve", lambda e: e.scalar_tensor_tensor(
                varow[0:1, :], PSTF[0:1, 0:512], 1.0 / D, msq[0:1, :],
                ALU.mult, ALU.subtract), inc="ve", waits=[evms2, st_rel])
            ps2_rel = evvr
            evlv = P.op("sc", lambda e: e.activation(varow[0:1, :], varow[0:1, :],
                                                     AF.Ln, bias=epscol[0:1, 0:1]),
                        inc="sc", waits=[evvr])
            evrs = P.op("sc", lambda e: e.activation(varow[0:1, :], varow[0:1, :],
                                                     AF.Exp, scale=-0.5),
                        inc="sc", waits=[evlv])
            evs1 = P.dma("sc", RROWD[0:1, :], murow[0:1, :], "lnb",
                         waits=[evmr, lnb_war])
            evs2 = P.dma("sc", RROWD[1:2, :], varow[0:1, :], "lnb",
                         waits=[evrs])
            lnb_war = evs2
            evc1 = P.dma("sc", bmu[:, :], RROWD[0:1, :].broadcast_to((128, 512)),
                         "lnb", waits=[evs1, bcn_rel])
            evc2_ = P.dma("sc", brs[:, :], RROWD[1:2, :].broadcast_to((128, 512)),
                          "lnb", waits=[evs2])
            bcn_rel = evc2_
            evn = None
            for dcb in range(KC):
                zch = zar.rearrange("p (c u) -> p c u", u=512)[:, dcb, :]
                evt = P.op("ve", lambda e, i=zch: e.tensor_sub(tmpn, i, bmu),
                           inc="ve", waits=[evc1, evc2_, evn])
                evn = P.op("ve", lambda e, o=zch: e.tensor_mul(o, tmpn, brs),
                           inc="ve", waits=[evt])
            out_sem = P.dma("sc", out3[:, :, qg * 512:(qg + 1) * 512],
                            zar.rearrange("p (c u) -> p c u", u=512), "out",
                            waits=[evn])
            fuse_rel = ("pe", P.cnt["pe"])
            x2f_war[qg % 2] = [fuse_rel, ("ve", P.cnt["ve"])]

        P.wait("sy", P.final("out"))
        P.wait("gp", P.final("out"))
        return _emit(nc, P, es)


def _emit(nc, P, es):
    sems = {}
    for name in P.cnt:
        sems[name] = es.enter_context(nc.semaphore(name))
    block = es.enter_context(nc.Block())

    def emit(eng_obj, lst):
        for item in lst:
            if item[0] == "w":
                _, name, v = item
                eng_obj.wait_ge(sems[name], v)
            else:
                _, fn, incn, n = item
                ins = fn(eng_obj)
                if incn is not None:
                    ins.then_inc(sems[incn], n)

    @block.sync
    def _(e):
        emit(e, P.ops["sy"])

    @block.tensor
    def _(e):
        emit(e, P.ops["te"])

    @block.vector
    def _(e):
        emit(e, P.ops["ve"])

    @block.scalar
    def _(e):
        emit(e, P.ops["sc"])

    @block.gpsimd
    def _(e):
        emit(e, P.ops["gp"])
    return nc


def kernel(L0_attn_x, L0_attn_wq, L0_attn_wk, L0_attn_wv, L0_attn_wo,
           L0_ffn_w_gate, L0_ffn_w_up, L0_ffn_w_down,
           L0_ffn_s_gate, L0_ffn_s_up, L0_ffn_s_down):
    cfg = CFG_FULL
    x = np.asarray(L0_attn_x)
    in_maps = make_host_inputs(
        cfg, x, np.asarray(L0_attn_wq), np.asarray(L0_attn_wk),
        np.asarray(L0_attn_wv), np.asarray(L0_attn_wo),
        np.asarray(L0_ffn_w_gate), np.asarray(L0_ffn_w_up),
        np.asarray(L0_ffn_w_down), np.asarray(L0_ffn_s_gate),
        np.asarray(L0_ffn_s_up), np.asarray(L0_ffn_s_down))
    nc = build_program(cfg)
    res = run_bass_kernel_spmd(nc, in_maps, core_ids=list(range(cfg["NCORES"])))
    return assemble_output(cfg, res.results)
